# version 20
# baseline (speedup 1.0000x reference)
import numpy as np
import ml_dtypes
import concourse.bass as bass
import concourse.mybir as mybir
from concourse.bass_utils import run_bass_kernel_spmd
from contextlib import ExitStack

F32 = mybir.dt.float32
BF16 = mybir.dt.bfloat16
FP8 = mybir.dt.float8e4
AF = mybir.ActivationFunctionType
ALU = mybir.AluOpType
AX = mybir.AxisListType

NCORE = 8
S = 2048
NT = 16
NB = 4
D = 1024
NSL = 1024
BT = 512
BF = ml_dtypes.bfloat16
E4 = ml_dtypes.float8_e4m3


class Sem:
    def __init__(self, stack, nc, name):
        self.h = stack.enter_context(nc.semaphore(name))
        self.n = 0

    def inc(self, instr, k=1):
        instr.then_inc(self.h, k)
        self.n += k
        return instr


def _vw(v, s_sv):
    class W:
        last = None

        def __getattr__(self, name):
            f = getattr(v, name)

            def wrap(*a, **k):
                i = f(*a, **k)
                W.last = i
                return i
            return wrap

        def sync(self):
            if W.last is not None:
                W.last.then_inc(s_sv.h, 1)
                s_sv.n += 1
                v.wait_ge(s_sv.h, s_sv.n)
                W.last = None
    return W()


def build_program(mode, taps=False):
    nc = bass.Bass("TRN2", target_bir_lowering=False, detect_race_conditions=False)
    ap = lambda *a, **k: nc.dram_tensor(*a, **k).ap()

    u_h = ap("u", [S, D], F32, kind="ExternalInput")
    a_rep_h = ap("a_rep", [128, 512], F32, kind="ExternalInput")
    omtab_h = ap("omtab", [128, 2048], F32, kind="ExternalInput")
    ident_h = ap("ident", [128, 128], BF16, kind="ExternalInput")
    if mode == "A":
        bw_h = ap("bw", [D, D], F32, kind="ExternalInput")
        cwt_h = ap("cwt", [D, D], F32, kind="ExternalInput")
        mbt_h = ap("mbt", [D, NSL], BF16, kind="ExternalInput")
        mbts_h = ap("mbts", [D, 128], BF16, kind="ExternalInput")
        mgts_h = ap("mgts", [D, 128], BF16, kind="ExternalInput")
        wtb_h = ap("wtb", [128, D], BF16, kind="ExternalInput")
        carr_o = ap("carr_out", [128, 8], F32, kind="ExternalOutput")
        if taps:
            tap_o = ap("tap_out", [8, 128, 1024], F32, kind="ExternalOutput")
        bsl_o = ap("bslice", [128, NSL], BF16, kind="ExternalOutput")
        gsl_o = ap("gslice", [128, D], BF16, kind="ExternalOutput")
        wsl_o = ap("wslice", [128, D], FP8, kind="ExternalOutput")
        tab_o = ap("tab_out", [8, 128, 2048], BF16, kind="ExternalOutput")
    else:
        bht_h = ap("bht", [8, 128, NSL], BF16, kind="ExternalInput")
        gt_h = ap("gt", [8, 128, D], BF16, kind="ExternalInput")
        wt8_h = ap("wt8", [8, 128, D], FP8, kind="ExternalInput")
        tab_h = ap("tab", [8, 128, 2048], BF16, kind="ExternalInput")
        dbc_h = ap("dbc", [128, D], F32, kind="ExternalInput")
        bias_h = ap("bias", [1, D], F32, kind="ExternalInput")
        ones_h = ap("ones_r", [1, 128], BF16, kind="ExternalInput")
        carr_h = ap("carr", [128, 64], F32, kind="ExternalInput")
        selm_h = ap("selm", [128, 256], BF16, kind="ExternalInput")
        ws_h = ap("ws_col", [128, 1], F32, kind="ExternalInput")
        y_o = ap("y", [S, D], F32, kind="ExternalOutput")

    with ExitStack() as stack:
        en = stack.enter_context
        sb = lambda name, shape, dt: en(nc.sbuf_tensor("sb_" + name, shape, dt))

        a_rep = sb("a_rep", [128, 512], F32)
        omtab = sb("omtab", [128, 2048], F32)
        ident = sb("ident", [128, 128], BF16)
        sc = {nm: sb("s_" + nm, [128, 4], F32) for nm in
              ["om", "lr", "li", "gr", "gi", "a", "b", "c", "d", "e", "f"]}
        sc_fill = sb("sc_fill", [128, 1], F32)
        pw_r = sb("pw_r", [128, 1], F32)
        pw_i = sb("pw_i", [128, 1], F32)
        PrP = sb("PrP", [128, 32], F32)
        PiP = sb("PiP", [128, 32], F32)
        ct = [sb(f"ct{m}", [128, 2048], BF16) for m in range(4)]
        stb = [sb(f"st{m}", [128, 2048], BF16) for m in range(4)]
        tmpa = sb("tmpa", [128, 512], F32)
        tmpb = sb("tmpb", [128, 512], F32)

        u_roll = [sb(f"u{i}", [128, D], F32) for i in range(4)]
        nxb = 2 if mode == "B" else 1
        xn_blk = [[sb(f"xn{b}_{i}", [128, D], BF16) for i in range(4)] for b in range(nxb)]
        xnT = [sb("xnT0", [128, 4096], BF16)]
        vbuf = [sb(f"vb{g}", [128, BT], BF16) for g in range(8)]
        vt = [sb(f"vt{g}", [128, BT], BF16) for g in range(8)]
        tb1 = sb("tb1", [128, BT], BF16)
        tb2 = sb("tb2", [128, BT], BF16)
        s0 = [sb(f"s0_{g}", [128, BT], BF16) for g in range(8)]
        zeros_bf = sb("zeros_bf", [128, BT], BF16)
        init_col = sb("init_col", [128, 8], F32)
        st_sum = sb("st_sum", [128, NT], F32)
        st_ssq = sb("st_ssq", [128, NT], F32)
        st_den = sb("st_den", [128, NT], F32)
        st_r0 = sb("st_r0", [128, NT], F32)
        st_y0 = sb("st_y0", [128, NT], F32)
        st_rstd = sb("st_rstd", [128, NT], F32)
        st_negmr = sb("st_negmr", [128, NT], F32)
        st_mean = sb("st_mean", [128, NT], F32)
        st_t1 = sb("st_t1", [128, NT], F32)
        st_t2 = sb("st_t2", [128, NT], F32)
        sq_dump = sb("sq_dump", [128, D], BF16)
        bht_sb = [sb(f"bht{d}", [128, NSL], BF16) for d in range(8)]

        if mode == "A":
            bw_sb = [sb(f"bw{q}", [128, D], BF16) for q in range(8)]
            bw_f = sb("bw_f", [128, D], F32)
            mbt_sb = [sb(f"mbt{q}", [128, NSL], BF16) for q in range(8)]
            cwt_sb = [sb(f"cwt{k}", [128, D], BF16) for k in range(8)]
            cwt_f = sb("cwt_f", [128, D], F32)
            mbts_sb = [sb(f"mbts{q}", [128, 128], BF16) for q in range(8)]
            mgts_sb = [sb(f"mgts{k}", [128, 128], BF16) for k in range(8)]
            wtb_sb = sb("wtb", [128, D], BF16)
            bsl_sb = sb("bsl", [128, NSL], BF16)
            gsl_sb = sb("gsl", [128, D], BF16)
            wsl_sb = sb("wsl", [128, D], FP8)
            carr_sb = sb("carr_sb", [128, 8], F32)
            ctf = sb("ctf", [128, 2048], F32)
            stf = sb("stf", [128, 2048], F32)
        else:
            gt_sb = [sb(f"gt{g}", [128, D], BF16) for g in range(8)]
            wt8_sb = [sb(f"wt8{d}", [128, D], FP8) for d in range(8)]
            dbc_sb = sb("dbc", [128, D], F32)
            bias_f = sb("bias_f", [1, D], F32)
            bias_b = sb("bias_b", [1, D], BF16)
            ones_sb = sb("ones_sb", [1, 128], BF16)
            carr_sb = sb("carr_sb", [128, 64], F32)
            selm_sb = sb("selm", [128, 256], BF16)
            ws_col = sb("ws_col", [128, 1], F32)
            qx_roll = [sb(f"qx{i}", [128, D], FP8) for i in range(2)]
            qxT = sb("qxT", [128, 4096], FP8)
            HT = [[sb(f"HT0_{g}", [128, BT], BF16) for g in range(8)]]
            fbuf = [sb(f"fb{i}", [128, D], F32) for i in range(4)]
            g1b = [sb("g1_0", [128, D], F32)]
            osb = [sb("os0", [128, D], F32)]
            st_amax = sb("st_amax", [128, NT], F32)
            st_qsc = sb("st_qsc", [128, NT], F32)
            st_negq = sb("st_negq", [128, NT], F32)
            st_sff = sb("st_sff", [128, NT], F32)
            c4 = {nm: sb("c_" + nm, [128, 4], F32) for nm in
                  ["mjr", "mji", "ta", "tb", "ar0", "ai0", "ar1", "ai1"]}
            t32a = sb("t32a", [128, 32], F32)
            t32b = sb("t32b", [128, 32], F32)

        ps_v = [en(nc.psum_tensor(f"psv{i}", [128, BT], F32)) for i in range(4)]
        ps_tx = en(nc.psum_tensor("ps_tx", [128, BT], F32))
        ps_tq = en(nc.psum_tensor("ps_tq", [128, BT], F32))
        ps_f = en(nc.psum_tensor("ps_f", [128, BT], F32))
        ps_m = en(nc.psum_tensor("ps_m", [128, BT], F32))

        sem = lambda nm: Sem(stack, nc, nm)
        s_w = sem("s_w"); s_u = sem("s_u"); s_st1 = sem("s_st1")
        s_stA = sem("s_stA"); s_stB = sem("s_stB"); s_stats = sem("s_stats")
        s_xn = sem("s_xn"); s_qs = sem("s_qs"); s_qx = sem("s_qx")
        s_trT = sem("s_trT"); s_trE = sem("s_trE")
        s_tqT = sem("s_tqT"); s_tqE = sem("s_tqE")
        s_vmm = sem("s_vmm"); s_vev = sem("s_vev"); s_pre = sem("s_pre")
        s_scan = sem("s_scan"); s_H = sem("s_H")
        s_mix = sem("s_mix"); s_o1 = sem("s_o1")
        s_ffn = sem("s_ffn"); s_fev = sem("s_fev"); s_ua = sem("s_ua")
        s_gp = sem("s_gp"); s_o2 = sem("s_o2"); s_set = sem("s_set"); s_sets = sem("s_sets")
        s_prep = sem("s_prep"); s_pev = sem("s_pev"); s_sv = sem("s_sv")
        s_stx = sem("s_stx")

        blk = en(nc.Block())

        @blk.sync
        def _(sy):
            def ld(dst, src):
                sy.dma_start(out=dst, in_=src).then_inc(s_w.h, 16)
                s_w.n += 16
            ld(a_rep[:], a_rep_h); ld(omtab[:], omtab_h); ld(ident[:], ident_h)
            if mode == "A":
                for q in range(8):
                    ld(mbt_sb[q][:], mbt_h[128 * q:128 * (q + 1), :])
                    ld(mbts_sb[q][:], mbts_h[128 * q:128 * (q + 1), :])
                    ld(mgts_sb[q][:], mgts_h[128 * q:128 * (q + 1), :])
                ld(wtb_sb[:], wtb_h)
            else:
                for d_ in range(8):
                    ld(bht_sb[d_][:], bht_h[d_])
                    ld(gt_sb[d_][:], gt_h[d_])
                    ld(wt8_sb[d_][:], wt8_h[d_])
                for m in range(4):
                    ld(ct[m][:], tab_h[m])
                    ld(stb[m][:], tab_h[4 + m])
                ld(dbc_sb[:], dbc_h); ld(bias_f[:], bias_h); ld(ones_sb[:], ones_h)
                ld(carr_sb[:], carr_h); ld(selm_sb[:], selm_h); ld(ws_col[:], ws_h)
            for tt in range(NT):
                if tt >= 4:
                    sy.wait_ge(s_xn.h, tt - 3)
                sy.dma_start(out=u_roll[tt % 4][:], in_=u_h[128 * tt:128 * (tt + 1), :]).then_inc(s_u.h, 16)
                s_u.n += 16
            if mode == "B":
                sy.wait_ge(s_stx.h, 16 * NT)
            else:
                sy.wait_ge(s_stx.h, 16 * 12)
        n_w = s_w.n

        if mode == "A":
            @blk.gpsimd
            def _(g):
                for q in range(8):
                    if q >= 1:
                        g.wait_ge(s_sets.h, q)
                    g.dma_start(out=bw_f[:], in_=bw_h[128 * q:128 * (q + 1), :]).then_inc(s_gp.h, 16)
                    s_gp.n += 16
                for k in range(8):
                    g.wait_ge(s_sets.h, 8 + k)
                    g.dma_start(out=cwt_f[:], in_=cwt_h[128 * k:128 * (k + 1), :]).then_inc(s_gp.h, 16)
                    s_gp.n += 16

            @blk.scalar
            def _(s):
                for q in range(8):
                    s.wait_ge(s_gp.h, 16 * (q + 1))
                    s_sets.inc(s.copy(out=bw_sb[q][:], in_=bw_f[:]))
                for k in range(8):
                    s.wait_ge(s_gp.h, 128 + 16 * (k + 1))
                    s_sets.inc(s.copy(out=cwt_sb[k][:], in_=cwt_f[:]))
                s.wait_ge(s_w.h, n_w)
                s_sets.inc(s.mul(out=wsl_sb[:], in_=wtb_sb[:], mul=0.5))

            @blk.tensor
            def _(te):
                te.wait_ge(s_sets.h, 16)
                te.wait_ge(s_w.h, n_w)
                for idx in range(16):
                    d_, h = idx // 2, idx % 2
                    if idx >= 1:
                        te.wait_ge(s_pev.h, idx)
                    for q in range(8):
                        mi = te.matmul(ps_m[:], bw_sb[q][:, 128 * d_:128 * (d_ + 1)],
                                       mbt_sb[q][:, BT * h:BT * (h + 1)],
                                       start=(q == 0), stop=(q == 7))
                    te.drain()
                    te.sem_inc(s_prep.h, 1)
                    s_prep.n += 1
                for d_ in range(8):
                    if d_ >= 1:
                        te.wait_ge(s_pev.h, 16 + d_)
                    for q in range(8):
                        mi = te.matmul(ps_tx[:, 0:128], bw_sb[q][:, 128 * d_:128 * (d_ + 1)],
                                       mbts_sb[q][:], start=(q == 0), stop=(q == 7))
                    te.drain()
                    te.sem_inc(s_prep.h, 1)
                    s_prep.n += 1
                for h in range(2):
                    if h >= 1:
                        te.wait_ge(s_pev.h, 24 + h)
                    for k in range(8):
                        mi = te.matmul(ps_tq[:], mgts_sb[k][:], cwt_sb[k][:, BT * h:BT * (h + 1)],
                                       start=(k == 0), stop=(k == 7))
                    te.drain()
                    te.sem_inc(s_prep.h, 1)
                    s_prep.n += 1

            @blk.scalar
            def _(s):
                for idx in range(16):
                    d_, h = idx // 2, idx % 2
                    s.wait_ge(s_prep.h, idx + 1)
                    s_pev.inc(s.copy(out=bht_sb[d_][:, BT * h:BT * (h + 1)], in_=ps_m[:]))
                for d_ in range(8):
                    s.wait_ge(s_prep.h, 17 + d_)
                    s_pev.inc(s.copy(out=bsl_sb[:, 128 * d_:128 * (d_ + 1)], in_=ps_tx[:, 0:128]))
                for h in range(2):
                    s.wait_ge(s_prep.h, 25 + h)
                    s_pev.inc(s.copy(out=gsl_sb[:, BT * h:BT * (h + 1)], in_=ps_tq[:]))

        @blk.vector
        def _(v):
            class VW:
                last = None

                def __getattr__(self, name):
                    f = getattr(v, name)

                    def wrap(*a, **k):
                        i = f(*a, **k)
                        VW.last = i
                        return i
                    return wrap

                def sync(self):
                    if VW.last is not None:
                        VW.last.then_inc(s_sv.h, 1)
                        s_sv.n += 1
                        v.wait_ge(s_sv.h, s_sv.n)
                        VW.last = None
            w = VW()

            def cmul4(or_, oi_, ar, ai, br, bi):
                w.tensor_tensor(out=sc["a"][:], in0=ar, in1=br, op=ALU.mult)
                w.tensor_tensor(out=sc["b"][:], in0=ai, in1=bi, op=ALU.mult)
                w.tensor_tensor(out=sc["c"][:], in0=ar, in1=bi, op=ALU.mult)
                w.tensor_tensor(out=sc["f"][:], in0=ai, in1=br, op=ALU.mult)
                w.sync()
                w.tensor_tensor(out=or_, in0=sc["a"][:], in1=sc["b"][:], op=ALU.subtract)
                w.tensor_tensor(out=oi_, in0=sc["c"][:], in1=sc["f"][:], op=ALU.add)
                w.sync()

            def renorm4(ar, ai):
                w.tensor_tensor(out=sc["a"][:], in0=ar, in1=ar, op=ALU.mult)
                w.tensor_tensor(out=sc["b"][:], in0=ai, in1=ai, op=ALU.mult)
                w.sync()
                w.tensor_tensor(out=sc["c"][:], in0=sc["a"][:], in1=sc["b"][:], op=ALU.add)
                w.sync()
                w.tensor_scalar(out=sc["c"][:], in0=sc["c"][:], scalar1=3.0, scalar2=-0.5,
                                op0=ALU.subtract, op1=ALU.mult)
                w.sync()
                w.tensor_tensor(out=sc["a"][:], in0=ar, in1=sc["c"][:], op=ALU.mult)
                w.tensor_tensor(out=sc["b"][:], in0=ai, in1=sc["c"][:], op=ALU.mult)
                w.sync()
                w.tensor_copy(out=ar, in_=sc["a"][:])
                w.tensor_copy(out=ai, in_=sc["b"][:])
                w.sync()

            v.wait_ge(s_w.h, n_w)
            w.memset(zeros_bf[:], 0.0)
            for m in range(4):
                w.tensor_tensor(out=tmpa[:], in0=omtab[:, 512 * m:512 * (m + 1)],
                                in1=a_rep[:], op=ALU.mult)
                w.sync()
                w.tensor_reduce(out=sc["om"][:, m:m + 1], in_=tmpa[:], axis=AX.X, op=ALU.add)
            w.sync()
            w.tensor_tensor(out=sc["a"][:], in0=sc["om"][:], in1=sc["om"][:], op=ALU.mult)
            w.sync()
            w.tensor_scalar(out=sc["b"][:], in0=sc["a"][:], scalar1=1.0, scalar2=None, op0=ALU.add)
            w.sync()
            w.reciprocal(out=sc["c"][:], in_=sc["b"][:])
            w.sync()
            w.tensor_tensor(out=sc["d"][:], in0=sc["b"][:], in1=sc["c"][:], op=ALU.mult)
            w.sync()
            w.tensor_scalar(out=sc["d"][:], in0=sc["d"][:], scalar1=2.0, scalar2=-1.0,
                            op0=ALU.subtract, op1=ALU.mult)
            w.sync()
            w.tensor_tensor(out=sc["e"][:], in0=sc["c"][:], in1=sc["d"][:], op=ALU.mult)
            w.tensor_scalar(out=sc["d"][:], in0=sc["a"][:], scalar1=1.0, scalar2=-1.0,
                            op0=ALU.subtract, op1=ALU.mult)
            w.sync()
            w.tensor_tensor(out=sc["lr"][:], in0=sc["d"][:], in1=sc["e"][:], op=ALU.mult)
            w.tensor_scalar(out=sc["d"][:], in0=sc["om"][:], scalar1=-2.0, scalar2=None, op0=ALU.mult)
            w.sync()
            w.tensor_tensor(out=sc["li"][:], in0=sc["d"][:], in1=sc["e"][:], op=ALU.mult)
            w.sync()

            if mode == "A":
                for m in range(4):
                    w.memset(ctf[:, 0:1], 1.0)
                    w.memset(stf[:, 0:1], 0.0)
                    w.tensor_copy(out=pw_r[:], in_=sc["lr"][:, m:m + 1])
                    w.tensor_copy(out=pw_i[:], in_=sc["li"][:, m:m + 1])
                    w.sync()
                    wlen = 1
                    while wlen < 2048:
                        subs = [(0, wlen)] if wlen <= 512 else [(j0, 512) for j0 in range(0, wlen, 512)]
                        for (j0, wsub) in subs:
                            w.tensor_scalar(out=tmpa[:, 0:wsub], in0=ctf[:, j0:j0 + wsub],
                                            scalar1=pw_r[:], scalar2=None, op0=ALU.mult)
                            w.tensor_scalar(out=tmpb[:, 0:wsub], in0=stf[:, j0:j0 + wsub],
                                            scalar1=pw_i[:], scalar2=None, op0=ALU.mult)
                            w.sync()
                            w.tensor_tensor(out=ctf[:, wlen + j0:wlen + j0 + wsub],
                                            in0=tmpa[:, 0:wsub], in1=tmpb[:, 0:wsub], op=ALU.subtract)
                            w.tensor_scalar(out=tmpa[:, 0:wsub], in0=ctf[:, j0:j0 + wsub],
                                            scalar1=pw_i[:], scalar2=None, op0=ALU.mult)
                            w.tensor_scalar(out=tmpb[:, 0:wsub], in0=stf[:, j0:j0 + wsub],
                                            scalar1=pw_r[:], scalar2=None, op0=ALU.mult)
                            w.sync()
                            w.tensor_tensor(out=stf[:, wlen + j0:wlen + j0 + wsub],
                                            in0=tmpa[:, 0:wsub], in1=tmpb[:, 0:wsub], op=ALU.add)
                        if wlen < 1024:
                            w.tensor_tensor(out=sc["a"][:, 0:1], in0=pw_r[:], in1=pw_r[:], op=ALU.mult)
                            w.tensor_tensor(out=sc["b"][:, 0:1], in0=pw_i[:], in1=pw_i[:], op=ALU.mult)
                            w.tensor_tensor(out=sc["c"][:, 0:1], in0=pw_r[:], in1=pw_i[:], op=ALU.mult)
                            w.sync()
                            w.tensor_tensor(out=pw_r[:], in0=sc["a"][:, 0:1], in1=sc["b"][:, 0:1], op=ALU.subtract)
                            w.tensor_scalar(out=pw_i[:], in0=sc["c"][:, 0:1], scalar1=2.0,
                                            scalar2=None, op0=ALU.mult)
                            w.sync()
                            w.tensor_tensor(out=sc["a"][:, 0:1], in0=pw_r[:], in1=pw_r[:], op=ALU.mult)
                            w.tensor_tensor(out=sc["b"][:, 0:1], in0=pw_i[:], in1=pw_i[:], op=ALU.mult)
                            w.sync()
                            w.tensor_tensor(out=sc["c"][:, 0:1], in0=sc["a"][:, 0:1], in1=sc["b"][:, 0:1], op=ALU.add)
                            w.sync()
                            w.tensor_scalar(out=sc["c"][:, 0:1], in0=sc["c"][:, 0:1], scalar1=3.0,
                                            scalar2=-0.5, op0=ALU.subtract, op1=ALU.mult)
                            w.sync()
                            w.tensor_tensor(out=sc["a"][:, 0:1], in0=pw_r[:], in1=sc["c"][:, 0:1], op=ALU.mult)
                            w.tensor_tensor(out=sc["b"][:, 0:1], in0=pw_i[:], in1=sc["c"][:, 0:1], op=ALU.mult)
                            w.sync()
                            w.tensor_copy(out=pw_r[:], in_=sc["a"][:, 0:1])
                            w.tensor_copy(out=pw_i[:], in_=sc["b"][:, 0:1])
                            w.sync()
                        wlen *= 2
                    w.sync()
                    w.tensor_copy(out=ct[m][:], in_=ctf[:])
                    w.tensor_copy(out=stb[m][:], in_=stf[:])
                v.drain()
                v.sem_inc(s_set.h, 1); s_set.n += 1

            w.tensor_copy(out=sc["gr"][:], in_=sc["lr"][:])
            w.tensor_copy(out=sc["gi"][:], in_=sc["li"][:])
            w.sync()
            for _ in range(11):
                cmul4(sc["d"][:], sc["e"][:], sc["gr"][:], sc["gi"][:], sc["gr"][:], sc["gi"][:])
                w.tensor_copy(out=sc["gr"][:], in_=sc["d"][:])
                w.tensor_copy(out=sc["gi"][:], in_=sc["e"][:])
                w.sync()
                renorm4(sc["gr"][:], sc["gi"][:])
            w.memset(PrP[:, 0:4], 1.0)
            w.memset(PiP[:, 0:4], 0.0)
            w.sync()
            for e_ in range(1, 8):
                cmul4(sc["d"][:], sc["e"][:], PrP[:, 4 * (e_ - 1):4 * e_], PiP[:, 4 * (e_ - 1):4 * e_],
                      sc["gr"][:], sc["gi"][:])
                w.tensor_copy(out=PrP[:, 4 * e_:4 * e_ + 4], in_=sc["d"][:])
                w.tensor_copy(out=PiP[:, 4 * e_:4 * e_ + 4], in_=sc["e"][:])
                w.sync()

            if mode == "B":
                for j in range(NCORE):
                    sel = selm_sb[:, 32 * j:32 * (j + 1)]
                    w.tensor_tensor(out=t32a[:], in0=PrP[:], in1=sel, op=ALU.mult)
                    w.tensor_tensor(out=t32b[:], in0=PiP[:], in1=sel, op=ALU.mult)
                    w.sync()
                    va = t32a[:, :].rearrange("p (e c) -> p c e", c=4)
                    vb = t32b[:, :].rearrange("p (e c) -> p c e", c=4)
                    w.tensor_reduce(out=c4["mjr"][:], in_=va, axis=AX.X, op=ALU.add)
                    w.tensor_reduce(out=c4["mji"][:], in_=vb, axis=AX.X, op=ALU.add)
                    rr = carr_sb[:, 8 * j:8 * j + 4]
                    ri = carr_sb[:, 8 * j + 4:8 * j + 8]
                    w.sync()
                    w.tensor_tensor(out=sc["a"][:], in0=c4["mjr"][:], in1=rr, op=ALU.mult)
                    w.tensor_tensor(out=sc["b"][:], in0=c4["mji"][:], in1=ri, op=ALU.mult)
                    w.tensor_tensor(out=sc["c"][:], in0=c4["mjr"][:], in1=ri, op=ALU.mult)
                    w.tensor_tensor(out=sc["f"][:], in0=c4["mji"][:], in1=rr, op=ALU.mult)
                    w.sync()
                    w.tensor_tensor(out=c4["ta"][:], in0=sc["a"][:], in1=sc["b"][:], op=ALU.subtract)
                    w.tensor_tensor(out=c4["tb"][:], in0=sc["c"][:], in1=sc["f"][:], op=ALU.add)
                    w.sync()
                    src_r = c4["ar0"] if j % 2 == 1 else c4["ar1"]
                    dst_r = c4["ar1"] if j % 2 == 1 else c4["ar0"]
                    src_i = c4["ai0"] if j % 2 == 1 else c4["ai1"]
                    dst_i = c4["ai1"] if j % 2 == 1 else c4["ai0"]
                    if j == 0:
                        w.tensor_copy(out=dst_r[:], in_=c4["ta"][:])
                        w.tensor_copy(out=dst_i[:], in_=c4["tb"][:])
                    else:
                        w.tensor_tensor(out=dst_r[:], in0=src_r[:], in1=c4["ta"][:], op=ALU.add)
                        w.tensor_tensor(out=dst_i[:], in0=src_i[:], in1=c4["tb"][:], op=ALU.add)
                    w.sync()
                fr = c4["ar1"] if (NCORE - 1) % 2 == 1 else c4["ar0"]
                fi = c4["ai1"] if (NCORE - 1) % 2 == 1 else c4["ai0"]
                w.tensor_copy(out=init_col[:, 0:4], in_=fr[:])
                w.tensor_copy(out=init_col[:, 4:8], in_=fi[:])
                v.drain()
                v.sem_inc(s_set.h, 1); s_set.n += 1

        if mode == "B":
            @blk.scalar
            def _(s):
                s.wait_ge(s_w.h, n_w)
                s_sets.inc(s.copy(out=bias_b[:], in_=bias_f[:]))

        nplane = 8
        for b in range(NB):
            bb = b % 2 if mode == "B" else 0
            tiles = [4 * b + i for i in range(4)]

            c4s = lambda arr: arr[:, 4 * b:4 * b + 4]

            @blk.vector
            def _(v, b=b, tiles=tiles):
                w = _vw(v, s_sv)
                for tau, tg in enumerate(tiles):
                    ub = u_roll[tg % 4]
                    v.wait_ge(s_u.h, 16 * (tg + 1))
                    w.tensor_tensor(out=sq_dump[:], in0=ub[:], in1=ub[:], op=ALU.mult)
                    w.tensor_reduce(out=st_sum[:, tg:tg + 1], in_=ub[:], axis=AX.X, op=ALU.add)
                    w.sync()
                    w.tensor_reduce(out=st_ssq[:, tg:tg + 1], in_=sq_dump[:], axis=AX.X, op=ALU.add)
                w.sync()
                C = lambda arr: arr[:, 4 * b:4 * b + 4]
                w.tensor_scalar(out=C(st_mean), in0=C(st_sum), scalar1=1.0 / D,
                                scalar2=None, op0=ALU.mult)
                w.tensor_scalar(out=C(st_t1), in0=C(st_ssq), scalar1=1.0 / D,
                                scalar2=None, op0=ALU.mult)
                w.sync()
                w.tensor_tensor(out=C(st_t2), in0=C(st_mean), in1=C(st_mean), op=ALU.mult)
                w.sync()
                w.scalar_tensor_tensor(out=C(st_den), in0=C(st_t2), scalar=-1.0,
                                       in1=C(st_t1), op0=ALU.mult, op1=ALU.add)
                w.sync()
                w.tensor_scalar(out=C(st_den), in0=C(st_den), scalar1=1e-5,
                                scalar2=None, op0=ALU.add)
                w.sync()
                w.tensor_scalar(out=C(st_rstd), in0=C(st_den), scalar1=3.0, scalar2=-0.5,
                                op0=ALU.subtract, op1=ALU.mult)
                w.sync()
                for _ in range(4):
                    w.tensor_tensor(out=C(st_t1), in0=C(st_rstd), in1=C(st_rstd), op=ALU.mult)
                    w.sync()
                    w.tensor_tensor(out=C(st_t2), in0=C(st_t1), in1=C(st_den), op=ALU.mult)
                    w.sync()
                    w.tensor_scalar(out=C(st_t2), in0=C(st_t2), scalar1=3.0, scalar2=-0.5,
                                    op0=ALU.subtract, op1=ALU.mult)
                    w.sync()
                    w.tensor_tensor(out=C(st_t1), in0=C(st_rstd), in1=C(st_t2), op=ALU.mult)
                    w.sync()
                    w.tensor_copy(out=C(st_rstd), in_=C(st_t1))
                    w.sync()
                w.tensor_tensor(out=C(st_t1), in0=C(st_mean), in1=C(st_rstd), op=ALU.mult)
                w.sync()
                w.tensor_scalar(out=C(st_negmr), in0=C(st_t1), scalar1=-1.0,
                                scalar2=None, op0=ALU.mult)
                w.sync()
                v.sem_inc(s_stats.h, 1); s_stats.n += 1

            @blk.scalar
            def _(s, b=b, tiles=tiles):
                s.wait_ge(s_stats.h, b + 1)
                for tau, tg in enumerate(tiles):
                    ub = u_roll[tg % 4]
                    s_xn.inc(s.activation(out=xn_blk[bb][tau][:], in_=ub[:], func=AF.Identity,
                                          scale=st_rstd[:, tg:tg + 1], bias=st_negmr[:, tg:tg + 1]))

            if mode == "B":
                @blk.vector
                def _(v, b=b, tiles=tiles):
                    v.wait_ge(s_xn.h, 4 * b + 4)
                    w = _vw(v, s_sv)
                    for tau, tg in enumerate(tiles):
                        w.tensor_reduce(out=st_amax[:, tg:tg + 1], in_=xn_blk[bb][tau][:],
                                        axis=AX.X, op=ALU.max, apply_absolute_value=True)
                    w.sync()
                    C = lambda arr: arr[:, 4 * b:4 * b + 4]
                    w.tensor_scalar(out=C(st_amax), in0=C(st_amax),
                                    scalar1=1e-12, scalar2=None, op0=ALU.max)
                    w.sync()
                    w.tensor_scalar(out=C(st_sff), in0=C(st_amax), scalar1=ws_col[:],
                                    scalar2=1.0 / 112.0, op0=ALU.mult, op1=ALU.mult)
                    w.reciprocal(out=C(st_t1), in_=C(st_amax))
                    w.sync()
                    w.tensor_scalar(out=C(st_qsc), in0=C(st_t1), scalar1=224.0,
                                    scalar2=None, op0=ALU.mult)
                    w.sync()
                    v.sem_inc(s_qs.h, 1); s_qs.n += 1

            for tau, tg in enumerate(tiles):
                @blk.tensor
                def _(te, tg=tg, bb=bb, tau=tau):
                    for grp in range(2):
                        gi = 2 * tg + grp
                        te.wait_ge(s_xn.h, tg + 1)
                        if mode == "A" and gi == 0:
                            te.wait_ge(s_pev.h, 26)
                        if gi >= 1:
                            te.wait_ge(s_trE.h, gi)
                        for k in range(4):
                            dd = 4 * grp + k
                            mi = te.matmul(ps_tx[:, 128 * k:128 * (k + 1)],
                                           xn_blk[bb][tau][:, 128 * dd:128 * (dd + 1)],
                                           ident[:], start=True, stop=True)
                        te.drain()
                        te.sem_inc(s_trT.h, 1)
                        s_trT.n += 1
                    if mode == "B":
                        for grp in range(2):
                            gi = 2 * tg + grp
                            te.wait_ge(s_qx.h, tg + 1)
                            if gi >= 1:
                                te.wait_ge(s_tqE.h, gi)
                            if tg >= 4:
                                te.wait_ge(s_ffn.h, 8 * (tg // 4))
                            for k in range(4):
                                dd = 4 * grp + k
                                mi = te.matmul(ps_tq[:, 128 * k:128 * (k + 1)],
                                               qx_roll[tg % 2][:, 128 * dd:128 * (dd + 1)],
                                               ident[:], start=True, stop=True)
                            te.drain()
                            te.sem_inc(s_tqT.h, 1)
                            s_tqT.n += 1

                @blk.scalar
                def _(s, tg=tg, b=b, bb=bb, tau=tau):
                    if mode == "B":
                        s.wait_ge(s_qs.h, b + 1)
                        if tg >= 2:
                            s.wait_ge(s_tqT.h, 2 * (tg - 1))
                        s_qx.inc(s.activation(out=qx_roll[tg % 2][:], in_=xn_blk[bb][tau][:],
                                              func=AF.Identity, scale=st_qsc[:, tg:tg + 1], bias=0.0))
                    for grp in range(2):
                        gi = 2 * tg + grp
                        s.wait_ge(s_trT.h, gi + 1)
                        if tg % 4 == 0 and tg >= 4 and grp == 0:
                            s.wait_ge(s_vmm.h, 4 * (tg // 4))
                        dst = xnT[0][:, :].rearrange("p (d c) -> p d c", d=8)
                        dst = dst[:, 4 * grp:4 * grp + 4, 128 * tau:128 * (tau + 1)]
                        s_trE.inc(s.copy(out=dst, in_=ps_tx[:, :].rearrange("p (d c) -> p d c", d=4)))
                    if mode == "B":
                        for grp in range(2):
                            gi = 2 * tg + grp
                            s.wait_ge(s_tqT.h, gi + 1)
                            dst = qxT[:, :].rearrange("p (d c) -> p d c", d=8)
                            dst = dst[:, 4 * grp:4 * grp + 4, 128 * tau:128 * (tau + 1)]
                            s_tqE.inc(s.copy(out=dst, in_=ps_tq[:, :].rearrange("p (d c) -> p d c", d=4)))

            if mode == "B":
                def emit_mix(bm):
                    @blk.tensor
                    def _(te, bm=bm):
                        te.wait_ge(s_H.h, bm + 1)
                        for tau in range(4):
                            for h in range(2):
                                idx = 8 * bm + 2 * tau + h
                                if idx >= 1:
                                    te.wait_ge(s_o1.h, idx)
                                for g_ in range(8):
                                    te.matmul(ps_m[:], HT[0][g_][:, 128 * tau:128 * (tau + 1)],
                                              gt_sb[g_][:, BT * h:BT * (h + 1)],
                                              start=(g_ == 0), stop=False)
                                mi = te.matmul(ps_m[:], ones_sb[:], bias_b[:, BT * h:BT * (h + 1)],
                                               start=False, stop=True)
                                te.drain()
                                te.sem_inc(s_mix.h, 1)
                                s_mix.n += 1

                    @blk.vector
                    def _(v, bm=bm):
                        for tau in range(4):
                            tg = 4 * bm + tau
                            for h in range(2):
                                idx = 8 * bm + 2 * tau + h
                                v.wait_ge(s_mix.h, idx + 1)
                                v.wait_ge(s_ua.h, 16 * (tg + 1))
                                if tg >= 1 and h == 0:
                                    v.wait_ge(s_o2.h, tg)
                                s_o1.inc(v.tensor_tensor(out=osb[0][:, BT * h:BT * (h + 1)],
                                                         in0=fbuf[tg % 4][:, BT * h:BT * (h + 1)],
                                                         in1=ps_m[:], op=ALU.add))

                    @blk.gpsimd
                    def _(g, bm=bm):
                        for tau in range(4):
                            tg = 4 * bm + tau
                            g.wait_ge(s_xn.h, tg + 1)
                            g.tensor_tensor(out=g1b[0][:], in0=xn_blk[bm % 2][tau][:],
                                            in1=dbc_sb[:], op=ALU.mult)
                            g.wait_ge(s_fev.h, 2 * tg + 2)
                            g.dma_start(out=fbuf[tg % 4][:], in_=u_h[128 * tg:128 * (tg + 1), :],
                                        accum_op=ALU.add).then_inc(s_ua.h, 16)
                            s_ua.n += 16
                            g.wait_ge(s_o1.h, 8 * bm + 2 * tau + 2)
                            s_o2.inc(g.tensor_tensor(out=fbuf[tg % 4][:], in0=osb[0][:],
                                                     in1=g1b[0][:], op=ALU.add))
                            g.dma_start(out=y_o[128 * tg:128 * (tg + 1), :],
                                        in_=fbuf[tg % 4][:]).then_inc(s_stx.h, 16)
                            s_stx.n += 16

                if b >= 1:
                    emit_mix(b - 1)

            @blk.tensor
            def _(te, b=b, bb=bb):
                te.wait_ge(s_trE.h, 8 * (b + 1))
                if mode == "A":
                    te.wait_ge(s_pev.h, 26)
                for p in range(4):
                    gpair = 4 * b + p
                    if gpair >= 2:
                        te.wait_ge(s_vev.h, 2 * (gpair - 1))
                    for half in range(2):
                        g_ = p + 4 * half
                        bank = ps_v[2 * (p % 2) + half]
                        for d_ in range(8):
                            mi = te.matmul(bank[:], bht_sb[d_][:, 128 * g_:128 * (g_ + 1)],
                                           xnT[0][:, 512 * d_:512 * (d_ + 1)],
                                           start=(d_ == 0), stop=(d_ == 7))
                    te.drain()
                    te.sem_inc(s_vmm.h, 1)
                    s_vmm.n += 1

            @blk.scalar
            def _(s, b=b):
                for p in range(4):
                    gpair = 4 * b + p
                    s.wait_ge(s_vmm.h, gpair + 1)
                    if b >= 1:
                        s.wait_ge(s_pre.h, 4 * (b - 1) + p + 1)
                    for half in range(2):
                        g_ = p + 4 * half
                        bank = ps_v[2 * (p % 2) + half]
                        ci = s.copy(out=vbuf[g_][:], in_=bank[:])
                    s_vev.inc(ci, 2)

            @blk.vector
            def _(v, b=b):
                for pp in range(2):
                    e_, o_ = 2 * pp, 2 * pp + 1
                    v.wait_ge(s_vev.h, 2 * (4 * b + o_ + 1))
                    csE = ct[e_][:, BT * b:BT * (b + 1)]
                    snE = stb[e_][:, BT * b:BT * (b + 1)]
                    csO = ct[o_][:, BT * b:BT * (b + 1)]
                    snO = stb[o_][:, BT * b:BT * (b + 1)]
                    vrE, viE = vbuf[e_][:], vbuf[e_ + 4][:]
                    vrO, viO = vbuf[o_][:], vbuf[o_ + 4][:]
                    v.tensor_tensor(out=tb1[:], in0=csE, in1=vrE, op=ALU.mult)
                    v.tensor_tensor(out=tb2[:], in0=csO, in1=vrO, op=ALU.mult)
                    v.tensor_tensor(out=vt[e_][:], in0=snE, in1=viE, op=ALU.mult)
                    v.tensor_tensor(out=vt[o_][:], in0=snO, in1=viO, op=ALU.mult)
                    v.tensor_tensor(out=vt[e_][:], in0=tb1[:], in1=vt[e_][:], op=ALU.add)
                    v.tensor_tensor(out=vt[o_][:], in0=tb2[:], in1=vt[o_][:], op=ALU.add)
                    v.tensor_tensor(out=vt[e_ + 4][:], in0=snE, in1=vrE, op=ALU.mult)
                    v.tensor_tensor(out=vt[o_ + 4][:], in0=snO, in1=vrO, op=ALU.mult)
                    v.tensor_tensor(out=tb1[:], in0=csE, in1=viE, op=ALU.mult)
                    v.tensor_tensor(out=tb2[:], in0=csO, in1=viO, op=ALU.mult)
                    v.tensor_tensor(out=vt[e_ + 4][:], in0=tb1[:], in1=vt[e_ + 4][:], op=ALU.subtract)
                    s_pre.inc(v.tensor_tensor(out=vt[o_ + 4][:], in0=tb2[:], in1=vt[o_ + 4][:],
                                              op=ALU.subtract), 2)
                if b == 0:
                    v.wait_ge(s_set.h, 1 if mode == "B" else 1)
                for g_ in range(8):
                    if b == 0 and mode == "B":
                        init = init_col[:, g_:g_ + 1]
                    elif b == 0:
                        init = 0.0
                    else:
                        init = s0[g_][:, BT - 1:BT]
                    si = v.tensor_tensor_scan(out=s0[g_][:], data0=vt[g_][:], data1=zeros_bf[:],
                                              initial=init, op0=ALU.add, op1=ALU.add)
                s_scan.inc(si)
                if mode == "B":
                    if b >= 1:
                        v.wait_ge(s_mix.h, 8 * b)
                    for pp in range(2):
                        e_, o_ = 2 * pp, 2 * pp + 1
                        csE = ct[e_][:, BT * b:BT * (b + 1)]
                        snE = stb[e_][:, BT * b:BT * (b + 1)]
                        csO = ct[o_][:, BT * b:BT * (b + 1)]
                        snO = stb[o_][:, BT * b:BT * (b + 1)]
                        srE, siE = s0[e_][:], s0[e_ + 4][:]
                        srO, siO = s0[o_][:], s0[o_ + 4][:]
                        hrE, hiE = HT[0][e_][:], HT[0][e_ + 4][:]
                        hrO, hiO = HT[0][o_][:], HT[0][o_ + 4][:]
                        v.tensor_tensor(out=hrE, in0=snE, in1=siE, op=ALU.mult)
                        v.tensor_tensor(out=hrO, in0=snO, in1=siO, op=ALU.mult)
                        v.tensor_tensor(out=tb1[:], in0=csE, in1=srE, op=ALU.mult)
                        v.tensor_tensor(out=tb2[:], in0=csO, in1=srO, op=ALU.mult)
                        v.tensor_tensor(out=hrE, in0=tb1[:], in1=hrE, op=ALU.subtract)
                        v.tensor_tensor(out=hrO, in0=tb2[:], in1=hrO, op=ALU.subtract)
                        v.tensor_tensor(out=hiE, in0=snE, in1=srE, op=ALU.mult)
                        v.tensor_tensor(out=hiO, in0=snO, in1=srO, op=ALU.mult)
                        v.tensor_tensor(out=tb1[:], in0=csE, in1=siE, op=ALU.mult)
                        v.tensor_tensor(out=tb2[:], in0=csO, in1=siO, op=ALU.mult)
                        v.tensor_tensor(out=hiE, in0=tb1[:], in1=hiE, op=ALU.add)
                        hlast = v.tensor_tensor(out=hiO, in0=tb2[:], in1=hiO, op=ALU.add)
                    s_H.inc(hlast)

            if mode == "B":
                @blk.tensor
                def _(te, b=b):
                    for tau in range(4):
                        tg = 4 * b + tau
                        for h in range(2):
                            idx = 8 * b + 2 * tau + h
                            te.wait_ge(s_tqE.h, 2 * tg + 2)
                            if idx >= 1:
                                te.wait_ge(s_fev.h, idx)
                            for d_ in range(8):
                                mi = te.matmul(ps_f[:], qxT[:, 512 * d_ + 128 * tau:512 * d_ + 128 * (tau + 1)],
                                               wt8_sb[d_][:, BT * h:BT * (h + 1)],
                                               start=(d_ == 0), stop=(d_ == 7))
                            te.drain()
                            te.sem_inc(s_ffn.h, 1)
                            s_ffn.n += 1

                @blk.scalar
                def _(s, b=b):
                    for tau in range(4):
                        tg = 4 * b + tau
                        for h in range(2):
                            idx = 8 * b + 2 * tau + h
                            s.wait_ge(s_ffn.h, idx + 1)
                            if tg >= 4:
                                s.wait_ge(s_o2.h, tg - 3)
                                s.wait_ge(s_stx.h, 16 * (tg - 3))
                            s_fev.inc(s.activation(out=fbuf[tg % 4][:, BT * h:BT * (h + 1)],
                                                   in_=ps_f[:], func=AF.Identity,
                                                   scale=st_sff[:, tg:tg + 1], bias=0.0))

                if b == NB - 1:
                    emit_mix(b)

        if mode == "A":
            @blk.vector
            def _(v):
                v.wait_ge(s_scan.h, NB)
                for g_ in range(8):
                    v.tensor_copy(out=carr_sb[:, g_:g_ + 1], in_=s0[g_][:, BT - 1:BT])
                v.drain()
                v.sem_inc(s_set.h, 1); s_set.n += 1

            @blk.gpsimd
            def _(g):
                g.wait_ge(s_set.h, 2)
                g.wait_ge(s_pev.h, 26)
                g.wait_ge(s_sets.h, 17)
                g.dma_start(out=carr_o, in_=carr_sb[:]).then_inc(s_stx.h, 16)
                g.dma_start(out=bsl_o, in_=bsl_sb[:]).then_inc(s_stx.h, 16)
                g.dma_start(out=gsl_o, in_=gsl_sb[:]).then_inc(s_stx.h, 16)
                g.dma_start(out=wsl_o, in_=wsl_sb[:]).then_inc(s_stx.h, 16)
                for m in range(4):
                    g.dma_start(out=tab_o[m], in_=ct[m][:]).then_inc(s_stx.h, 16)
                    g.dma_start(out=tab_o[4 + m], in_=stb[m][:]).then_inc(s_stx.h, 16)
                s_stx.n += 16 * 12
                n_tap = 0
                if taps:
                    for tsrc in [st_sum, st_den, st_r0, st_y0, st_rstd, st_negmr]:
                        g.dma_start(out=tap_o[n_tap][:, 0:NT], in_=tsrc[:]).then_inc(s_stx.h, 16)
                        n_tap += 1
                    g.dma_start(out=tap_o[6][:, 0:512], in_=vbuf[0][:]).then_inc(s_stx.h, 16)
                    g.dma_start(out=tap_o[7][:, 0:512], in_=s0[0][:]).then_inc(s_stx.h, 16)
                    s_stx.n += 16 * 8
                g.wait_ge(s_stx.h, 16 * (12 + (8 if taps else 0)))

    return nc



_CONST = {}


def _consts():
    if _CONST:
        return _CONST
    N = 1024
    s_idx = np.arange(NSL)
    f_of = np.where(s_idx <= 512, s_idx, s_idx - 512)
    q = np.arange(N)
    ang = 2.0 * np.pi * np.outer(f_of, q) / N
    MB = np.where((s_idx <= 512)[:, None], np.cos(ang), -np.sin(ang))
    alpha = np.where((f_of == 0) | (f_of == 512), 1.0, 2.0)
    MG = np.where((s_idx <= 512)[:, None], np.cos(ang), -np.sin(ang)) * alpha[:, None] / N
    _CONST["MBT"] = np.ascontiguousarray(MB.T).astype(BF)
    _CONST["MGT"] = np.ascontiguousarray(MG.T).astype(BF)
    p = np.arange(128)
    om = np.zeros((128, 2048), np.float32)
    for m in range(4):
        s_pair = 128 * m + p
        jj = np.arange(512)
        val = -2.0 * np.sin(2.0 * np.pi * np.outer(s_pair, jj + 1) / N)
        val[:, 511] = 0.0
        om[:, 512 * m:512 * (m + 1)] = val
    _CONST["omtab"] = om
    _CONST["ident"] = np.eye(128).astype(BF)
    _CONST["ones_r"] = np.ones((1, 128), BF)
    _CONST["ncF"] = build_fused()
    return _CONST


def kernel(L0_attn_u, L0_attn_a_params, L0_attn_B_w, L0_attn_C_w, L0_attn_D,
           L0_ffn_weight, L0_ffn_weight_scale, L0_ffn_bias):
    C = _consts()
    u = np.ascontiguousarray(np.asarray(L0_attn_u, np.float32).reshape(16384, D))
    a_p = np.asarray(L0_attn_a_params, np.float32)
    a_rep = np.zeros((128, 512), np.float32)
    a_rep[:, :511] = a_p[None, :511]
    Bw = np.ascontiguousarray(np.asarray(L0_attn_B_w, np.float32))
    CwT = np.ascontiguousarray(np.asarray(L0_attn_C_w, np.float32).T)
    WT_bf = np.ascontiguousarray(
        np.asarray(L0_ffn_weight).astype(BF).T)
    ws = float(np.asarray(L0_ffn_weight_scale, np.float32))
    bias = np.asarray(L0_ffn_bias, np.float32).reshape(1, D)
    dbc = np.broadcast_to(np.asarray(L0_attn_D, np.float32), (128, D)).copy()
    ws_col = np.zeros((128, 4), np.float32)
    ws_col[:, 0] = ws; ws_col[:, 1] = 1.0; ws_col[:, 3] = 0.5
    uf = np.ascontiguousarray(u[:7 * S])

    in_F = []
    for i in range(NCORE):
        selm = np.zeros((128, 256), np.float32)
        for j in range(NCORE):
            e_ = i - j
            if 1 <= e_ <= 7:
                for c_ in range(4):
                    selm[:, 32 * j + 4 * e_ + c_] = 1.0
        in_F.append(dict(
            u=u[S * i:S * (i + 1)], uf=uf,
            a_rep=a_rep, omtab=C["omtab"], ident=C["ident"],
            bw=Bw, cwt=CwT, mbt=C["MBT"], mgt=C["MGT"], wtb=WT_bf,
            dbc=dbc, bias=bias, ones_r=C["ones_r"],
            selm=selm.astype(BF), ws_col=ws_col,
        ))
    res = run_bass_kernel_spmd(C["ncF"], in_F, core_ids=list(range(NCORE)))
    y = np.concatenate([np.asarray(res.results[i]["y"], np.float32) for i in range(NCORE)], axis=0)
    return y.reshape(1, 16384, D)



def build_fused():
    nc = bass.Bass("TRN2", target_bir_lowering=False, detect_race_conditions=False)
    ap = lambda *a, **k: nc.dram_tensor(*a, **k).ap()

    u_h = ap("u", [S, D], F32, kind="ExternalInput")
    uf_h = ap("uf", [7 * S, D], F32, kind="ExternalInput")
    a_rep_h = ap("a_rep", [128, 512], F32, kind="ExternalInput")
    omtab_h = ap("omtab", [128, 2048], F32, kind="ExternalInput")
    ident_h = ap("ident", [128, 128], BF16, kind="ExternalInput")
    bw_h = ap("bw", [D, D], F32, kind="ExternalInput")
    cwt_h = ap("cwt", [D, D], F32, kind="ExternalInput")
    mbt_h = ap("mbt", [D, NSL], BF16, kind="ExternalInput")
    mgt_h = ap("mgt", [D, NSL], BF16, kind="ExternalInput")
    wtb_h = ap("wtb", [D, D], BF16, kind="ExternalInput")
    dbc_h = ap("dbc", [128, D], F32, kind="ExternalInput")
    bias_h = ap("bias", [1, D], F32, kind="ExternalInput")
    ones_h = ap("ones_r", [1, 128], BF16, kind="ExternalInput")
    selm_h = ap("selm", [128, 256], BF16, kind="ExternalInput")
    ws_h = ap("ws_col", [128, 4], F32, kind="ExternalInput")
    y_o = ap("y", [S, D], F32, kind="ExternalOutput")

    NCB = 8
    NTI = 16 * NCB

    with ExitStack() as stack:
        en = stack.enter_context
        sb = lambda name, shape, dt: en(nc.sbuf_tensor("sb_" + name, shape, dt))

        ident = sb("ident", [128, 128], BF16)
        sc = {nm: sb("s_" + nm, [128, 4], F32) for nm in
              ["om", "lr", "li", "gr", "gi", "a", "b", "c", "d", "e", "f"]}
        pw_r = sb("pw_r", [128, 4], F32)
        pw_i = sb("pw_i", [128, 4], F32)
        pw_ni = sb("pw_ni", [128, 4], F32)
        PrP = sb("PrP", [128, 32], F32)
        PiP = sb("PiP", [128, 32], F32)
        ct = [sb(f"ct{m}", [128, 2048], BF16) for m in range(4)]
        stb = [sb(f"st{m}", [128, 2048], BF16) for m in range(4)]
        init_col = sb("init_col", [128, 8], F32)
        st_sum = sb("st_sum", [128, NT], F32)
        st_ssq = sb("st_ssq", [128, NT], F32)
        st_den = sb("st_den", [128, NT], F32)
        st_rstd = sb("st_rstd", [128, NT], F32)
        st_negmr = sb("st_negmr", [128, NT], F32)
        st_mean = sb("st_mean", [128, NT], F32)
        st_t1 = sb("st_t1", [128, NT], F32)
        st_t2 = sb("st_t2", [128, NT], F32)
        st_amax = sb("st_amax", [128, NT], F32)
        st_qsc = sb("st_qsc", [128, NT], F32)
        st_sff = sb("st_sff", [128, NT], F32)
        sq_dump = [sb(f"sq_dump{i}", [128, D], BF16) for i in range(4)]
        bht_sb = [sb(f"bht{d}", [128, NSL], BF16) for d in range(8)]
        gt_sb = [sb(f"gt{g}", [128, D], BF16) for g in range(8)]
        wt8_sb = [sb(f"wt8{d}", [128, D], FP8) for d in range(8)]
        dbc_sb = sb("dbc", [128, D], BF16)
        bias_b = sb("bias_b", [1, D], BF16)
        ones_sb = sb("ones_sb", [1, 128], BF16)
        carr_sb = sb("carr_sb", [128, 64], F32)
        selm_sb = sb("selm", [128, 256], BF16)
        ws_col = sb("ws_col", [128, 4], F32)
        c4 = {nm: sb("c_" + nm, [128, 4], F32) for nm in
              ["mjr", "mji", "ta", "tb", "ar0", "ai0", "ar1", "ai1"]}
        t32a = sb("t32a", [128, 32], F32)
        t32b = sb("t32b", [128, 32], F32)
        rtmp = sb("rtmp", [128, 8], F32)
        racc = [sb(f"racc{i}", [128, 8], F32) for i in range(2)]

        ph0 = ExitStack()
        sb0 = lambda name, shape, dt: ph0.enter_context(nc.sbuf_tensor("sb_" + name, shape, dt))
        a_rep = sb0("a_rep", [128, 512], F32)
        omtab = sb0("omtab", [128, 2048], F32)
        tmpa = sb0("tmpa", [128, 512], F32)
        ctf = [sb0(f"ctf{m}", [128, 2048], F32) for m in range(4)]
        stf = [sb0(f"stf{m}", [128, 2048], F32) for m in range(4)]
        wf = [sb0(f"wf{i}", [128, D], F32) for i in range(2)]
        wstage = [sb0(f"wstg{q}", [128, D], BF16) for q in range(8)]
        mstage = [sb0(f"mstg{q}", [128, NSL], BF16) for q in range(8)]

        ps_v = [en(nc.psum_tensor(f"psv{i}", [128, BT], F32)) for i in range(4)]
        ps_tx = en(nc.psum_tensor("ps_tx", [128, BT], F32))
        ps_tq = en(nc.psum_tensor("ps_tq", [128, BT], F32))
        ps_f = en(nc.psum_tensor("ps_f", [128, BT], F32))
        ps_m = en(nc.psum_tensor("ps_m", [128, BT], F32))

        sem = lambda nm: Sem(stack, nc, nm)
        s_w = sem("s_w"); s_u = sem("s_u"); s_stats = sem("s_stats")
        s_xn = sem("s_xn"); s_qs = sem("s_qs"); s_qx = sem("s_qx")
        s_trT = sem("s_trT"); s_trE = sem("s_trE")
        s_tqT = sem("s_tqT"); s_tqE = sem("s_tqE")
        s_vmm = sem("s_vmm"); s_vev = sem("s_vev"); s_pre = sem("s_pre")
        s_scan = sem("s_scan"); s_H = sem("s_H")
        s_mix = sem("s_mix"); s_o1 = sem("s_o1")
        s_ffn = sem("s_ffn"); s_fev = sem("s_fev"); s_ua = sem("s_ua")
        s_gp = sem("s_gp"); s_o2 = sem("s_o2"); s_set = sem("s_set"); s_sets = sem("s_sets")
        s_prep = sem("s_prep"); s_pev = sem("s_pev"); s_sv = sem("s_sv")
        s_gsum = sem("s_gsum"); s_gpr2 = sem("s_gpr2"); s_rdc = sem("s_rdc")
        s_stx = sem("s_stx")

        blk = en(nc.Block())

        @blk.sync
        def _(sy):
            def ld(dst, src):
                sy.dma_start(out=dst, in_=src).then_inc(s_w.h, 16)
                s_w.n += 16
            ld(a_rep[:], a_rep_h); ld(omtab[:], omtab_h)
            ld(ident[:], ident_h)
            ld(ones_sb[:], ones_h); ld(selm_sb[:], selm_h); ld(ws_col[:], ws_h)
            for q in range(8):
                ld(mstage[q][:], mbt_h[128 * q:128 * (q + 1), :])
            sy.wait_ge(s_prep.h, 16)
            for k in range(8):
                ld(mstage[k][:], mgt_h[128 * k:128 * (k + 1), :])
            sy.wait_ge(s_prep.h, 32)
            for d_ in range(8):
                if d_ >= 2:
                    sy.wait_ge(s_sets.h, 16 + d_ - 1)
                ld(mstage[d_ % 2][:], wtb_h[128 * d_:128 * (d_ + 1), :])
        n_w0 = s_w.n

        @blk.gpsimd
        def _(g):
            for q in range(8):
                if q >= 2:
                    g.wait_ge(s_sets.h, q - 1)
                g.dma_start(out=wf[q % 2][:], in_=bw_h[128 * q:128 * (q + 1), :]).then_inc(s_gp.h, 16)
                s_gp.n += 16
            for k in range(8):
                g.wait_ge(s_sets.h, 7 + k)
                g.dma_start(out=wf[k % 2][:], in_=cwt_h[128 * k:128 * (k + 1), :]).then_inc(s_gp.h, 16)
                s_gp.n += 16
            g.wait_ge(s_sets.h, 15)
            g.dma_start(out=wf[0][:], in_=dbc_h).then_inc(s_gp.h, 16)
            s_gp.n += 16
            g.wait_ge(s_sets.h, 16)
            g.dma_start(out=wf[1][0:1, :], in_=bias_h).then_inc(s_gp.h, 16)
            s_gp.n += 16

        @blk.scalar
        def _(s):
            s.wait_ge(s_w.h, 96)
            one_c, zero_c, half_c = ws_col[:, 1:2], ws_col[:, 2:3], ws_col[:, 3:4]
            for q in range(8):
                s.wait_ge(s_gp.h, 16 * (q + 1))
                s_sets.inc(s.activation(out=wstage[q][:], in_=wf[q % 2][:], func=AF.Identity, scale=one_c, bias=zero_c))
            for idx in range(16):
                d_, h = idx // 2, idx % 2
                bank = ps_m if idx % 2 == 0 else ps_tx
                s.wait_ge(s_prep.h, idx + 1)
                s_pev.inc(s.activation(out=bht_sb[d_][:, BT * h:BT * (h + 1)], in_=bank[:], func=AF.Identity, scale=one_c, bias=zero_c))
            for k in range(8):
                s.wait_ge(s_gp.h, 128 + 16 * (k + 1))
                s_sets.inc(s.activation(out=wstage[k][:], in_=wf[k % 2][:], func=AF.Identity, scale=one_c, bias=zero_c))
            for idx in range(16):
                g_, h = idx // 2, idx % 2
                bank = ps_tq if idx % 2 == 0 else ps_f
                s.wait_ge(s_prep.h, 17 + idx)
                s_pev.inc(s.activation(out=gt_sb[g_][:, BT * h:BT * (h + 1)], in_=bank[:], func=AF.Identity, scale=one_c, bias=zero_c))
            for d_ in range(8):
                s.wait_ge(s_w.h, 352 + 16 * (d_ + 1))
                s_sets.inc(s.activation(out=wt8_sb[d_][:], in_=mstage[d_ % 2][:], func=AF.Identity, scale=half_c, bias=zero_c))
            s.wait_ge(s_w.h, 128)
            s.wait_ge(s_gp.h, 16 * 17)
            s_sets.inc(s.activation(out=dbc_sb[:], in_=wf[0][:], func=AF.Identity,
                                    scale=one_c, bias=zero_c))
            s.wait_ge(s_gp.h, 16 * 18)
            s_sets.inc(s.activation(out=bias_b[:], in_=wf[1][0:1, :], func=AF.Identity,
                                    scale=ws_col[0:1, 1:2], bias=ws_col[0:1, 2:3]))

        @blk.tensor
        def _(te):
            te.wait_ge(s_sets.h, 8)
            te.wait_ge(s_w.h, 224)
            for idx in range(16):
                d_, h = idx // 2, idx % 2
                bank = ps_m if idx % 2 == 0 else ps_tx
                if idx >= 2:
                    te.wait_ge(s_pev.h, idx - 1)
                for q in range(8):
                    mi = te.matmul(bank[:], wstage[q][:, 128 * d_:128 * (d_ + 1)],
                                   mstage[q][:, BT * h:BT * (h + 1)],
                                   start=(q == 0), stop=(q == 7))
                s_prep.inc(mi)
            te.wait_ge(s_sets.h, 16)
            te.wait_ge(s_w.h, 352)
            for idx in range(16):
                g_, h = idx // 2, idx % 2
                bank = ps_tq if idx % 2 == 0 else ps_f
                if idx >= 2:
                    te.wait_ge(s_pev.h, 16 + idx - 1)
                for k in range(8):
                    mi = te.matmul(bank[:], mstage[k][:, 128 * g_:128 * (g_ + 1)],
                                   wstage[k][:, BT * h:BT * (h + 1)],
                                   start=(k == 0), stop=(k == 7))
                s_prep.inc(mi)

        @blk.vector
        def _(v):
            w = _vw(v, s_sv)

            def cmul4(or_, oi_, ar, ai, br, bi):
                w.tensor_tensor(out=sc["a"][:], in0=ar, in1=br, op=ALU.mult)
                w.tensor_tensor(out=sc["b"][:], in0=ai, in1=bi, op=ALU.mult)
                w.tensor_tensor(out=sc["c"][:], in0=ar, in1=bi, op=ALU.mult)
                w.tensor_tensor(out=sc["f"][:], in0=ai, in1=br, op=ALU.mult)
                w.sync()
                w.tensor_tensor(out=or_, in0=sc["a"][:], in1=sc["b"][:], op=ALU.subtract)
                w.tensor_tensor(out=oi_, in0=sc["c"][:], in1=sc["f"][:], op=ALU.add)
                w.sync()

            def renorm4(ar, ai):
                w.tensor_tensor(out=sc["a"][:], in0=ar, in1=ar, op=ALU.mult)
                w.tensor_tensor(out=sc["b"][:], in0=ai, in1=ai, op=ALU.mult)
                w.sync()
                w.tensor_tensor(out=sc["c"][:], in0=sc["a"][:], in1=sc["b"][:], op=ALU.add)
                w.sync()
                w.tensor_scalar(out=sc["c"][:], in0=sc["c"][:], scalar1=3.0, scalar2=-0.5,
                                op0=ALU.subtract, op1=ALU.mult)
                w.sync()
                w.tensor_tensor(out=sc["a"][:], in0=ar, in1=sc["c"][:], op=ALU.mult)
                w.tensor_tensor(out=sc["b"][:], in0=ai, in1=sc["c"][:], op=ALU.mult)
                w.sync()
                w.tensor_copy(out=ar, in_=sc["a"][:])
                w.tensor_copy(out=ai, in_=sc["b"][:])
                w.sync()

            v.wait_ge(s_w.h, 32)
            w.memset(carr_sb[:, 56:64], 0.0)
            for m in range(4):
                w.tensor_tensor(out=tmpa[:], in0=omtab[:, 512 * m:512 * (m + 1)],
                                in1=a_rep[:], op=ALU.mult)
                w.sync()
                w.tensor_reduce(out=sc["om"][:, m:m + 1], in_=tmpa[:], axis=AX.X, op=ALU.add)
            w.sync()
            w.tensor_tensor(out=sc["a"][:], in0=sc["om"][:], in1=sc["om"][:], op=ALU.mult)
            w.sync()
            w.tensor_scalar(out=sc["b"][:], in0=sc["a"][:], scalar1=1.0, scalar2=None, op0=ALU.add)
            w.sync()
            w.reciprocal(out=sc["c"][:], in_=sc["b"][:])
            w.sync()
            w.tensor_tensor(out=sc["d"][:], in0=sc["b"][:], in1=sc["c"][:], op=ALU.mult)
            w.sync()
            w.tensor_scalar(out=sc["d"][:], in0=sc["d"][:], scalar1=2.0, scalar2=-1.0,
                            op0=ALU.subtract, op1=ALU.mult)
            w.sync()
            w.tensor_tensor(out=sc["e"][:], in0=sc["c"][:], in1=sc["d"][:], op=ALU.mult)
            w.tensor_scalar(out=sc["d"][:], in0=sc["a"][:], scalar1=1.0, scalar2=-1.0,
                            op0=ALU.subtract, op1=ALU.mult)
            w.sync()
            w.tensor_tensor(out=sc["lr"][:], in0=sc["d"][:], in1=sc["e"][:], op=ALU.mult)
            w.tensor_scalar(out=sc["d"][:], in0=sc["om"][:], scalar1=-2.0, scalar2=None, op0=ALU.mult)
            w.sync()
            w.tensor_tensor(out=sc["li"][:], in0=sc["d"][:], in1=sc["e"][:], op=ALU.mult)
            w.sync()

            for m in range(4):
                w.memset(ctf[m][:, 0:1], 1.0)
                w.memset(stf[m][:, 0:1], 0.0)
            w.tensor_copy(out=pw_r[:], in_=sc["lr"][:])
            w.tensor_copy(out=pw_i[:], in_=sc["li"][:])
            w.sync()
            w.tensor_scalar(out=pw_ni[:], in0=pw_i[:], scalar1=-1.0, scalar2=None, op0=ALU.mult)
            w.sync()
            wlen = 1
            while wlen < 2048:
                subs = [(0, wlen)] if wlen <= 512 else [(j0, 512) for j0 in range(0, wlen, 512)]
                for (j0, wsub) in subs:
                    for m in range(4):
                        w.tensor_scalar(out=ctf[m][:, wlen + j0:wlen + j0 + wsub],
                                        in0=ctf[m][:, j0:j0 + wsub],
                                        scalar1=pw_r[:, m:m + 1], scalar2=None, op0=ALU.mult)
                        w.tensor_scalar(out=stf[m][:, wlen + j0:wlen + j0 + wsub],
                                        in0=stf[m][:, j0:j0 + wsub],
                                        scalar1=pw_r[:, m:m + 1], scalar2=None, op0=ALU.mult)
                    for m in range(4):
                        w.scalar_tensor_tensor(out=ctf[m][:, wlen + j0:wlen + j0 + wsub],
                                               in0=stf[m][:, j0:j0 + wsub],
                                               scalar=pw_ni[:, m:m + 1],
                                               in1=ctf[m][:, wlen + j0:wlen + j0 + wsub],
                                               op0=ALU.mult, op1=ALU.add)
                        w.scalar_tensor_tensor(out=stf[m][:, wlen + j0:wlen + j0 + wsub],
                                               in0=ctf[m][:, j0:j0 + wsub],
                                               scalar=pw_i[:, m:m + 1],
                                               in1=stf[m][:, wlen + j0:wlen + j0 + wsub],
                                               op0=ALU.mult, op1=ALU.add)
                    w.sync()
                if wlen < 1024:
                    w.tensor_tensor(out=sc["a"][:], in0=pw_r[:], in1=pw_r[:], op=ALU.mult)
                    w.tensor_tensor(out=sc["b"][:], in0=pw_i[:], in1=pw_i[:], op=ALU.mult)
                    w.tensor_tensor(out=sc["c"][:], in0=pw_r[:], in1=pw_i[:], op=ALU.mult)
                    w.sync()
                    w.tensor_tensor(out=pw_r[:], in0=sc["a"][:], in1=sc["b"][:], op=ALU.subtract)
                    w.tensor_scalar(out=pw_i[:], in0=sc["c"][:], scalar1=2.0,
                                    scalar2=None, op0=ALU.mult)
                    w.sync()
                    w.tensor_scalar(out=pw_ni[:], in0=pw_i[:], scalar1=-1.0, scalar2=None, op0=ALU.mult)
                    w.sync()
                wlen *= 2
            w.sync()
            for m in range(4):
                w.tensor_copy(out=ct[m][:], in_=ctf[m][:])
                w.tensor_copy(out=stb[m][:], in_=stf[m][:])
            v.drain()
            v.sem_inc(s_set.h, 1); s_set.n += 1

            w.tensor_copy(out=sc["gr"][:], in_=sc["lr"][:])
            w.tensor_copy(out=sc["gi"][:], in_=sc["li"][:])
            w.sync()
            for _ in range(11):
                cmul4(sc["d"][:], sc["e"][:], sc["gr"][:], sc["gi"][:], sc["gr"][:], sc["gi"][:])
                w.tensor_copy(out=sc["gr"][:], in_=sc["d"][:])
                w.tensor_copy(out=sc["gi"][:], in_=sc["e"][:])
                w.sync()
            w.memset(PrP[:, 0:4], 1.0)
            w.memset(PiP[:, 0:4], 0.0)
            w.sync()
            for e_ in range(1, 8):
                cmul4(sc["d"][:], sc["e"][:], PrP[:, 4 * (e_ - 1):4 * e_], PiP[:, 4 * (e_ - 1):4 * e_],
                      sc["gr"][:], sc["gi"][:])
                w.tensor_copy(out=PrP[:, 4 * e_:4 * e_ + 4], in_=sc["d"][:])
                w.tensor_copy(out=PiP[:, 4 * e_:4 * e_ + 4], in_=sc["e"][:])
                w.sync()

        ph0.close()
        u_roll = [sb(f"u{i}", [128, D], F32) for i in range(4)]
        xn_blk = [[sb(f"xn{b}_{i}", [128, D], BF16) for i in range(4)] for b in range(2)]
        xnT = [sb(f"xnT{i}", [128, 4096], BF16) for i in range(2)]
        qx_roll = [sb(f"qx{i}", [128, D], FP8) for i in range(2)]
        qxT = sb("qxT", [128, 4096], FP8)
        vbuf = [sb(f"vb{g}", [128, BT], BF16) for g in range(8)]
        vt = [sb(f"vt{g}", [128, BT], BF16) for g in range(8)]
        tb1 = sb("tb1", [128, BT], BF16)
        tb2 = sb("tb2", [128, BT], BF16)
        tb3 = sb("tb3", [128, BT], BF16)
        tb4 = sb("tb4", [128, BT], BF16)
        s0 = [sb(f"s0_{g}", [128, BT], BF16) for g in range(8)]
        HT = [sb(f"HT0_{g}", [128, BT], BF16) for g in range(8)]
        fbuf = [sb(f"fb{i}", [128, D], F32) for i in range(4)]
        g1b = sb("g1_0", [128, D], BF16)
        osb = [sb(f"os{i}", [128, D], F32) for i in range(2)]

        @blk.sync
        def _(sy):
            sy.wait_ge(s_pev.h, 32)
            sy.wait_ge(s_sets.h, 26)
            sy.wait_ge(s_set.h, 1)
            for tt in range(NTI):
                if tt >= 4:
                    sy.wait_ge(s_xn.h, tt - 3)
                src = uf_h[128 * tt:128 * (tt + 1), :] if tt < 112 else \
                    u_h[128 * (tt - 112):128 * (tt - 111), :]
                sy.dma_start(out=u_roll[tt % 4][:], in_=src).then_inc(s_u.h, 16)
                s_u.n += 16
            sy.wait_ge(s_stx.h, 16 * NT)

        def emit_mix(bm):
            @blk.tensor
            def _(te, bm=bm):
                te.wait_ge(s_H.h, bm + 1)
                for tau in range(4):
                    for h in range(2):
                        idx = 8 * bm + 2 * tau + h
                        if idx >= 2:
                            te.wait_ge(s_o1.h, idx - 1)
                        bank = ps_v[2 + idx % 2]
                        for g_ in range(8):
                            te.matmul(bank[:], HT[g_][:, 128 * tau:128 * (tau + 1)],
                                      gt_sb[g_][:, BT * h:BT * (h + 1)],
                                      start=(g_ == 0), stop=False)
                        mi = te.matmul(bank[:], ones_sb[:], bias_b[:, BT * h:BT * (h + 1)],
                                       start=False, stop=True)
                        s_mix.inc(mi)

            @blk.vector
            def _(v, bm=bm):
                for tau in range(4):
                    tg = 4 * bm + tau
                    for h in range(2):
                        idx = 8 * bm + 2 * tau + h
                        v.wait_ge(s_mix.h, idx + 1)
                        v.wait_ge(s_ua.h, 16 * (tg + 1))
                        if tg >= 2 and h == 0:
                            v.wait_ge(s_o2.h, tg - 1)
                        s_o1.inc(v.tensor_tensor(out=osb[tg % 2][:, BT * h:BT * (h + 1)],
                                                 in0=fbuf[tg % 4][:, BT * h:BT * (h + 1)],
                                                 in1=ps_v[2 + idx % 2][:], op=ALU.add))

            @blk.gpsimd
            def _(g, bm=bm):
                for tau in range(4):
                    tg = 4 * bm + tau
                    g.wait_ge(s_xn.h, 112 + tg + 1)
                    g.tensor_tensor(out=g1b[:], in0=xn_blk[bm % 2][tau][:],
                                    in1=dbc_sb[:], op=ALU.mult)
                    g.wait_ge(s_fev.h, 2 * tg + 2)
                    g.dma_start(out=fbuf[tg % 4][:], in_=u_h[128 * tg:128 * (tg + 1), :],
                                accum_op=ALU.add).then_inc(s_ua.h, 16)
                    s_ua.n += 16
                    g.wait_ge(s_o1.h, 8 * bm + 2 * tau + 2)
                    s_o2.inc(g.tensor_tensor(out=fbuf[tg % 4][:], in0=osb[tg % 2][:],
                                             in1=g1b[:], op=ALU.add))
                    g.dma_start(out=y_o[128 * tg:128 * (tg + 1), :],
                                in_=fbuf[tg % 4][:]).then_inc(s_stx.h, 16)
                    s_stx.n += 16

        pending_prescale = []
        for cb in range(NCB):
            own = (cb == NCB - 1)
            for b_l in range(NB):
                gb = NB * cb + b_l
                bb = gb % 2
                tiles_l = [4 * b_l + i for i in range(4)]

                @blk.gpsimd
                def _(g, cb=cb, gb=gb, tiles_l=tiles_l):
                        if gb >= 1:
                            g.wait_ge(s_stats.h, gb)
                        for tau, tl in enumerate(tiles_l):
                            ti = 16 * cb + tl
                            g.wait_ge(s_u.h, 16 * (ti + 1))
                            ub = u_roll[ti % 4]
                            s_gsum.inc(g.tensor_tensor(out=sq_dump[tau][:], in0=ub[:],
                                                       in1=ub[:], op=ALU.mult))

                @blk.vector
                def _(v, cb=cb, b_l=b_l, gb=gb, own=own, tiles_l=tiles_l):
                    w = _vw(v, s_sv)
                    if gb >= 4:
                        v.wait_ge(s_xn.h, 4 * gb - 12)
                    for tau, tl in enumerate(tiles_l):
                        ti = 16 * cb + tl
                        v.wait_ge(s_u.h, 16 * (ti + 1))
                        v.wait_ge(s_gsum.h, 4 * gb + tau + 1)
                        ub = u_roll[ti % 4]
                        v.tensor_reduce(out=st_sum[:, tl:tl + 1], in_=ub[:], axis=AX.X, op=ALU.add)
                        wl = v.tensor_reduce(out=st_ssq[:, tl:tl + 1], in_=sq_dump[tau][:],
                                             axis=AX.X, op=ALU.add)
                    wl.then_inc(s_sv.h, 1); s_sv.n += 1
                    v.wait_ge(s_sv.h, s_sv.n)
                    w.sync()
                    C = lambda arr: arr[:, 4 * b_l:4 * b_l + 4]
                    w.tensor_scalar(out=C(st_mean), in0=C(st_sum), scalar1=1.0 / D,
                                    scalar2=None, op0=ALU.mult)
                    w.tensor_scalar(out=C(st_t1), in0=C(st_ssq), scalar1=1.0 / D,
                                    scalar2=1e-5, op0=ALU.mult, op1=ALU.add)
                    w.sync()
                    w.tensor_tensor(out=C(st_t2), in0=C(st_mean), in1=C(st_mean), op=ALU.mult)
                    w.sync()
                    w.scalar_tensor_tensor(out=C(st_den), in0=C(st_t2), scalar=-1.0,
                                           in1=C(st_t1), op0=ALU.mult, op1=ALU.add)
                    w.sync()
                    w.tensor_scalar(out=C(st_rstd), in0=C(st_den), scalar1=3.0, scalar2=-0.5,
                                    op0=ALU.subtract, op1=ALU.mult)
                    w.sync()
                    for _ in range(2):
                        w.tensor_tensor(out=C(st_t1), in0=C(st_rstd), in1=C(st_rstd), op=ALU.mult)
                        w.sync()
                        w.tensor_tensor(out=C(st_t2), in0=C(st_t1), in1=C(st_den), op=ALU.mult)
                        w.sync()
                        w.tensor_scalar(out=C(st_t2), in0=C(st_t2), scalar1=3.0, scalar2=-0.5,
                                        op0=ALU.subtract, op1=ALU.mult)
                        w.sync()
                        w.tensor_tensor(out=C(st_t1), in0=C(st_rstd), in1=C(st_t2), op=ALU.mult)
                        w.sync()
                        w.tensor_copy(out=C(st_rstd), in_=C(st_t1))
                        w.sync()
                    w.scalar_tensor_tensor(out=C(st_negmr), in0=C(st_mean), scalar=-1.0,
                                           in1=C(st_rstd), op0=ALU.mult, op1=ALU.mult)
                    w.sync()
                    v.sem_inc(s_stats.h, 1); s_stats.n += 1
                while pending_prescale and (len(pending_prescale) > 1 or gb >= 1):
                    pending_prescale.pop(0)()

                @blk.scalar
                def _(s, gb=gb, bb=bb, tiles_l=tiles_l):
                    s.wait_ge(s_stats.h, gb + 1)
                    for tau, tl in enumerate(tiles_l):
                        ti = 16 * (gb // 4) + tl
                        ub = u_roll[ti % 4]
                        s_xn.inc(s.activation(out=xn_blk[bb][tau][:], in_=ub[:], func=AF.Identity,
                                              scale=st_rstd[:, tl:tl + 1], bias=st_negmr[:, tl:tl + 1]))

                if own:
                    @blk.vector
                    def _(v, b_l=b_l, bb=bb, tiles_l=tiles_l):
                        v.wait_ge(s_xn.h, 112 + 4 * b_l + 4)
                        w = _vw(v, s_sv)
                        for tau, tl in enumerate(tiles_l):
                            w.tensor_reduce(out=st_amax[:, tl:tl + 1], in_=xn_blk[bb][tau][:],
                                            axis=AX.X, op=ALU.max, apply_absolute_value=True)
                        w.sync()
                        C = lambda arr: arr[:, 4 * b_l:4 * b_l + 4]
                        w.tensor_scalar(out=C(st_amax), in0=C(st_amax),
                                        scalar1=1e-12, scalar2=None, op0=ALU.max)
                        w.sync()
                        w.tensor_scalar(out=C(st_sff), in0=C(st_amax), scalar1=ws_col[:, 0:1],
                                        scalar2=1.0 / 112.0, op0=ALU.mult, op1=ALU.mult)
                        w.reciprocal(out=C(st_t1), in_=C(st_amax))
                        w.sync()
                        w.tensor_scalar(out=C(st_qsc), in0=C(st_t1), scalar1=224.0,
                                        scalar2=None, op0=ALU.mult)
                        w.sync()
                        v.sem_inc(s_qs.h, 1); s_qs.n += 1

                for tau, tl in enumerate(tiles_l):
                    ti = 16 * cb + tl

                    @blk.tensor
                    def _(te, ti=ti, tl=tl, bb=bb, tau=tau, own=own, b_l=b_l):
                        for grp in range(2):
                            gi = 2 * ti + grp
                            bank = ps_tx if (own or gi % 2 == 0) else ps_tq
                            te.wait_ge(s_xn.h, ti + 1)
                            if own and gi >= 1:
                                te.wait_ge(s_trE.h, gi)
                            elif gi >= 2:
                                te.wait_ge(s_trE.h, gi - 1)
                            for k in range(4):
                                dd = 4 * grp + k
                                mi = te.matmul(bank[:, 128 * k:128 * (k + 1)],
                                               xn_blk[bb][tau][:, 128 * dd:128 * (dd + 1)],
                                               ident[:], start=True, stop=True)
                            s_trT.inc(mi)
                        if own:
                            for grp in range(2):
                                gl = 2 * tl + grp
                                te.wait_ge(s_qx.h, tl + 1)
                                if gl == 0 and tl == 0:
                                    te.wait_ge(s_trE.h, 224)
                                if gl >= 1:
                                    te.wait_ge(s_tqE.h, gl)
                                if tl >= 4:
                                    te.wait_ge(s_ffn.h, 8 * (tl // 4))
                                    te.wait_ge(s_fev.h, 8 * (tl // 4))
                                for k in range(4):
                                    dd = 4 * grp + k
                                    mi = te.matmul(ps_tq[:, 128 * k:128 * (k + 1)],
                                                   qx_roll[tl % 2][:, 128 * dd:128 * (dd + 1)],
                                                   ident[:], start=True, stop=True)
                                s_tqT.inc(mi)

                    @blk.scalar
                    def _(s, ti=ti, tl=tl, bb=bb, tau=tau, own=own, b_l=b_l):
                        if own:
                            s.wait_ge(s_qs.h, b_l + 1)
                            if tl >= 2:
                                s.wait_ge(s_tqT.h, 2 * (tl - 1))
                            s_qx.inc(s.activation(out=qx_roll[tl % 2][:], in_=xn_blk[bb][tau][:],
                                                  func=AF.Identity, scale=st_qsc[:, tl:tl + 1], bias=ws_col[:, 2:3]))
                        for grp in range(2):
                            gi = 2 * ti + grp
                            bank_tr = ps_tx if (own or gi % 2 == 0) else ps_tq
                            s.wait_ge(s_trT.h, gi + 1)
                            if ti % 4 == 0 and ti >= 8 and grp == 0:
                                s.wait_ge(s_vmm.h, 4 * (ti // 4 - 1))
                            dst = xnT[(ti // 4) % 2][:, :].rearrange("p (d c) -> p d c", d=8)
                            dst = dst[:, 4 * grp:4 * grp + 4, 128 * tau:128 * (tau + 1)]
                            s_trE.inc(s.activation(out=dst, in_=bank_tr[:, :].rearrange("p (d c) -> p d c", d=4),
                                                   func=AF.Identity, scale=ws_col[:, 1:2], bias=ws_col[:, 2:3]))
                        if own:
                            for grp in range(2):
                                gl = 2 * tl + grp
                                s.wait_ge(s_tqT.h, gl + 1)
                                dst = qxT[:, :].rearrange("p (d c) -> p d c", d=8)
                                dst = dst[:, 4 * grp:4 * grp + 4, 128 * tau:128 * (tau + 1)]
                                s_tqE.inc(s.activation(out=dst, in_=ps_tq[:, :].rearrange("p (d c) -> p d c", d=4),
                                                       func=AF.Identity, scale=ws_col[:, 1:2], bias=ws_col[:, 2:3]))

                if own and b_l >= 1:
                    emit_mix(b_l - 1)

                @blk.tensor
                def _(te, gb=gb, own=own, b_l=b_l):
                    te.wait_ge(s_trE.h, 8 * (gb + 1))
                    for p in range(4):
                        gpair = 4 * gb + p
                        if own and p == 1 and b_l >= 1:
                            te.wait_ge(s_o1.h, 8 * b_l)
                        if gpair >= 2:
                            te.wait_ge(s_vev.h, 2 * (gpair - 1))
                        for half in range(2):
                            g_ = p + 4 * half
                            bank = ps_v[2 * (p % 2) + half]
                            for d_ in range(8):
                                mi = te.matmul(bank[:], bht_sb[d_][:, 128 * g_:128 * (g_ + 1)],
                                               xnT[gb % 2][:, 512 * d_:512 * (d_ + 1)],
                                               start=(d_ == 0), stop=(d_ == 7))
                        s_vmm.inc(mi)

                @blk.scalar
                def _(s, gb=gb):
                    for p in range(4):
                        gpair = 4 * gb + p
                        s.wait_ge(s_vmm.h, gpair + 1)
                        if gb >= 1:
                            s.wait_ge(s_pre.h, 4 * (gb - 1) + p + 1)
                        for half in range(2):
                            g_ = p + 4 * half
                            bank = ps_v[2 * (p % 2) + half]
                            ci = s.activation(out=vbuf[g_][:], in_=bank[:],
                                              func=AF.Identity, scale=ws_col[:, 1:2], bias=ws_col[:, 2:3])
                        s_vev.inc(ci, 2)

                def emit_prescale(cb=cb, b_l=b_l, gb=gb, own=own):
                    @blk.vector
                    def _(v, cb=cb, b_l=b_l, gb=gb, own=own):
                      for pp in range(2):
                          e_, o_ = 2 * pp, 2 * pp + 1
                          v.wait_ge(s_vev.h, 2 * (4 * gb + o_ + 1))
                          csE = ct[e_][:, BT * b_l:BT * (b_l + 1)]
                          snE = stb[e_][:, BT * b_l:BT * (b_l + 1)]
                          csO = ct[o_][:, BT * b_l:BT * (b_l + 1)]
                          snO = stb[o_][:, BT * b_l:BT * (b_l + 1)]
                          vrE, viE = vbuf[e_][:], vbuf[e_ + 4][:]
                          vrO, viO = vbuf[o_][:], vbuf[o_ + 4][:]
                          v.tensor_tensor(out=tb1[:], in0=csE, in1=vrE, op=ALU.mult)
                          v.tensor_tensor(out=tb2[:], in0=csO, in1=vrO, op=ALU.mult)
                          v.tensor_tensor(out=vt[e_][:], in0=snE, in1=viE, op=ALU.mult)
                          v.tensor_tensor(out=vt[o_][:], in0=snO, in1=viO, op=ALU.mult)
                          v.tensor_tensor(out=vt[e_][:], in0=tb1[:], in1=vt[e_][:], op=ALU.add)
                          v.tensor_tensor(out=vt[o_][:], in0=tb2[:], in1=vt[o_][:], op=ALU.add)
                          v.tensor_tensor(out=vt[e_ + 4][:], in0=snE, in1=vrE, op=ALU.mult)
                          v.tensor_tensor(out=vt[o_ + 4][:], in0=snO, in1=vrO, op=ALU.mult)
                          v.tensor_tensor(out=tb1[:], in0=csE, in1=viE, op=ALU.mult)
                          v.tensor_tensor(out=tb2[:], in0=csO, in1=viO, op=ALU.mult)
                          v.tensor_tensor(out=vt[e_ + 4][:], in0=tb1[:], in1=vt[e_ + 4][:], op=ALU.subtract)
                          s_pre.inc(v.tensor_tensor(out=vt[o_ + 4][:], in0=tb2[:], in1=vt[o_ + 4][:],
                                                    op=ALU.subtract), 2)
                      if not own:
                          for g_ in range(8):
                              ri = v.tensor_reduce(out=rtmp[:, g_:g_ + 1], in_=vt[g_][:],
                                                   axis=AX.X, op=ALU.add)
                          ri.then_inc(s_sv.h, 1); s_sv.n += 1
                          v.wait_ge(s_sv.h, s_sv.n)
                          if b_l == 0:
                              v.tensor_copy(out=racc[0][:], in_=rtmp[:])
                          elif b_l < NB - 1:
                              v.tensor_tensor(out=racc[b_l % 2][:], in0=racc[(b_l + 1) % 2][:],
                                              in1=rtmp[:], op=ALU.add)
                          else:
                              v.tensor_tensor(out=carr_sb[:, 8 * cb:8 * cb + 8],
                                              in0=racc[(b_l + 1) % 2][:], in1=rtmp[:], op=ALU.add)
                      else:
                          for g_ in range(8):
                              init = init_col[:, g_:g_ + 1] if b_l == 0 else s0[g_][:, BT - 1:BT]
                              v.tensor_tensor_scan(out=s0[g_][:], data0=vt[g_][:], data1=vt[g_][:],
                                                   initial=init, op0=ALU.add, op1=ALU.bypass)
                      if own:
                          if b_l >= 1:
                              v.wait_ge(s_mix.h, 8 * b_l)
                          for pp in range(2):
                              e_, o_ = 2 * pp, 2 * pp + 1
                              csE = ct[e_][:, BT * b_l:BT * (b_l + 1)]
                              snE = stb[e_][:, BT * b_l:BT * (b_l + 1)]
                              csO = ct[o_][:, BT * b_l:BT * (b_l + 1)]
                              snO = stb[o_][:, BT * b_l:BT * (b_l + 1)]
                              srE, siE = s0[e_][:], s0[e_ + 4][:]
                              srO, siO = s0[o_][:], s0[o_ + 4][:]
                              hrE, hiE = HT[e_][:], HT[e_ + 4][:]
                              hrO, hiO = HT[o_][:], HT[o_ + 4][:]
                              v.tensor_tensor(out=hrE, in0=snE, in1=siE, op=ALU.mult)
                              v.tensor_tensor(out=hrO, in0=snO, in1=siO, op=ALU.mult)
                              v.tensor_tensor(out=tb1[:], in0=csE, in1=srE, op=ALU.mult)
                              v.tensor_tensor(out=tb2[:], in0=csO, in1=srO, op=ALU.mult)
                              v.tensor_tensor(out=hrE, in0=tb1[:], in1=hrE, op=ALU.subtract)
                              v.tensor_tensor(out=hrO, in0=tb2[:], in1=hrO, op=ALU.subtract)
                              v.tensor_tensor(out=hiE, in0=snE, in1=srE, op=ALU.mult)
                              v.tensor_tensor(out=hiO, in0=snO, in1=srO, op=ALU.mult)
                              v.tensor_tensor(out=tb1[:], in0=csE, in1=siE, op=ALU.mult)
                              v.tensor_tensor(out=tb2[:], in0=csO, in1=siO, op=ALU.mult)
                              v.tensor_tensor(out=hiE, in0=tb1[:], in1=hiE, op=ALU.add)
                              hlast = v.tensor_tensor(out=hiO, in0=tb2[:], in1=hiO, op=ALU.add)
                          s_H.inc(hlast)

                pending_prescale.append(emit_prescale)


                if own:
                    @blk.tensor
                    def _(te, b_l=b_l):
                        for tau in range(4):
                            tl = 4 * b_l + tau
                            for h in range(2):
                                idx = 8 * b_l + 2 * tau + h
                                te.wait_ge(s_tqE.h, 2 * tl + 2)
                                if idx >= 2:
                                    te.wait_ge(s_fev.h, idx - 1)
                                fbank = ps_f if idx % 2 == 0 else ps_tq
                                for d_ in range(8):
                                    mi = te.matmul(fbank[:], qxT[:, 512 * d_ + 128 * tau:512 * d_ + 128 * (tau + 1)],
                                                   wt8_sb[d_][:, BT * h:BT * (h + 1)],
                                                   start=(d_ == 0), stop=(d_ == 7))
                                s_ffn.inc(mi)

                    @blk.scalar
                    def _(s, b_l=b_l):
                        for tau in range(4):
                            tl = 4 * b_l + tau
                            for h in range(2):
                                idx = 8 * b_l + 2 * tau + h
                                s.wait_ge(s_ffn.h, idx + 1)
                                if tl >= 4:
                                    s.wait_ge(s_o2.h, tl - 3)
                                    s.wait_ge(s_stx.h, 16 * (tl - 3))
                                s_fev.inc(s.activation(out=fbuf[tl % 4][:, BT * h:BT * (h + 1)],
                                                       in_=(ps_f if idx % 2 == 0 else ps_tq)[:],
                                                       func=AF.Identity,
                                                       scale=st_sff[:, tl:tl + 1], bias=ws_col[:, 2:3]))

                    if b_l == NB - 1:
                        while pending_prescale:
                            pending_prescale.pop(0)()
                        emit_mix(b_l)

            if cb == NCB - 2:
                while pending_prescale:
                    pending_prescale.pop(0)()
                @blk.vector
                def _(v):
                    w = _vw(v, s_sv)
                    for j in range(NCORE):
                        sel = selm_sb[:, 32 * j:32 * (j + 1)]
                        w.tensor_tensor(out=t32a[:], in0=PrP[:], in1=sel, op=ALU.mult)
                        w.tensor_tensor(out=t32b[:], in0=PiP[:], in1=sel, op=ALU.mult)
                        w.sync()
                        va = t32a[:, :].rearrange("p (e c) -> p c e", c=4)
                        vb = t32b[:, :].rearrange("p (e c) -> p c e", c=4)
                        w.tensor_reduce(out=c4["mjr"][:], in_=va, axis=AX.X, op=ALU.add)
                        w.tensor_reduce(out=c4["mji"][:], in_=vb, axis=AX.X, op=ALU.add)
                        rr = carr_sb[:, 8 * j:8 * j + 4]
                        ri = carr_sb[:, 8 * j + 4:8 * j + 8]
                        w.sync()
                        w.tensor_tensor(out=sc["a"][:], in0=c4["mjr"][:], in1=rr, op=ALU.mult)
                        w.tensor_tensor(out=sc["b"][:], in0=c4["mji"][:], in1=ri, op=ALU.mult)
                        w.tensor_tensor(out=sc["c"][:], in0=c4["mjr"][:], in1=ri, op=ALU.mult)
                        w.tensor_tensor(out=sc["f"][:], in0=c4["mji"][:], in1=rr, op=ALU.mult)
                        w.sync()
                        w.tensor_tensor(out=c4["ta"][:], in0=sc["a"][:], in1=sc["b"][:], op=ALU.subtract)
                        w.tensor_tensor(out=c4["tb"][:], in0=sc["c"][:], in1=sc["f"][:], op=ALU.add)
                        w.sync()
                        src_r = c4["ar0"] if j % 2 == 1 else c4["ar1"]
                        dst_r = c4["ar1"] if j % 2 == 1 else c4["ar0"]
                        src_i = c4["ai0"] if j % 2 == 1 else c4["ai1"]
                        dst_i = c4["ai1"] if j % 2 == 1 else c4["ai0"]
                        if j == 0:
                            w.tensor_copy(out=dst_r[:], in_=c4["ta"][:])
                            w.tensor_copy(out=dst_i[:], in_=c4["tb"][:])
                        else:
                            w.tensor_tensor(out=dst_r[:], in0=src_r[:], in1=c4["ta"][:], op=ALU.add)
                            w.tensor_tensor(out=dst_i[:], in0=src_i[:], in1=c4["tb"][:], op=ALU.add)
                        w.sync()
                    fr = c4["ar1"] if (NCORE - 1) % 2 == 1 else c4["ar0"]
                    fi = c4["ai1"] if (NCORE - 1) % 2 == 1 else c4["ai0"]
                    w.tensor_copy(out=init_col[:, 0:4], in_=fr[:])
                    w.tensor_copy(out=init_col[:, 4:8], in_=fi[:])
                    v.drain()
                    v.sem_inc(s_set.h, 1); s_set.n += 1

    return nc


# version 27
# speedup vs baseline: 1.0157x; 1.0157x over previous
import numpy as np
import ml_dtypes
import concourse.bass as bass
import concourse.mybir as mybir
from concourse.bass_utils import run_bass_kernel_spmd
from contextlib import ExitStack

F32 = mybir.dt.float32
BF16 = mybir.dt.bfloat16
FP8 = mybir.dt.float8e4
AF = mybir.ActivationFunctionType
ALU = mybir.AluOpType
AX = mybir.AxisListType

NCORE = 8
S = 2048
NT = 16
NB = 4
D = 1024
NSL = 1024
BT = 512
BF = ml_dtypes.bfloat16
E4 = ml_dtypes.float8_e4m3


class Sem:
    def __init__(self, stack, nc, name):
        self.h = stack.enter_context(nc.semaphore(name))
        self.n = 0

    def inc(self, instr, k=1):
        instr.then_inc(self.h, k)
        self.n += k
        return instr


def _vw(v, s_sv):
    class W:
        last = None

        def __getattr__(self, name):
            f = getattr(v, name)

            def wrap(*a, **k):
                i = f(*a, **k)
                W.last = i
                return i
            return wrap

        def sync(self):
            if W.last is not None:
                W.last.then_inc(s_sv.h, 1)
                s_sv.n += 1
                v.wait_ge(s_sv.h, s_sv.n)
                W.last = None
    return W()


def build_program(mode, taps=False):
    nc = bass.Bass("TRN2", target_bir_lowering=False, detect_race_conditions=False)
    ap = lambda *a, **k: nc.dram_tensor(*a, **k).ap()

    u_h = ap("u", [S, D], F32, kind="ExternalInput")
    a_rep_h = ap("a_rep", [128, 512], F32, kind="ExternalInput")
    omtab_h = ap("omtab", [128, 2048], F32, kind="ExternalInput")
    ident_h = ap("ident", [128, 128], BF16, kind="ExternalInput")
    if mode == "A":
        bw_h = ap("bw", [D, D], F32, kind="ExternalInput")
        cwt_h = ap("cwt", [D, D], F32, kind="ExternalInput")
        mbt_h = ap("mbt", [D, NSL], BF16, kind="ExternalInput")
        mbts_h = ap("mbts", [D, 128], BF16, kind="ExternalInput")
        mgts_h = ap("mgts", [D, 128], BF16, kind="ExternalInput")
        wtb_h = ap("wtb", [128, D], BF16, kind="ExternalInput")
        carr_o = ap("carr_out", [128, 8], F32, kind="ExternalOutput")
        if taps:
            tap_o = ap("tap_out", [8, 128, 1024], F32, kind="ExternalOutput")
        bsl_o = ap("bslice", [128, NSL], BF16, kind="ExternalOutput")
        gsl_o = ap("gslice", [128, D], BF16, kind="ExternalOutput")
        wsl_o = ap("wslice", [128, D], FP8, kind="ExternalOutput")
        tab_o = ap("tab_out", [8, 128, 2048], BF16, kind="ExternalOutput")
    else:
        bht_h = ap("bht", [8, 128, NSL], BF16, kind="ExternalInput")
        gt_h = ap("gt", [8, 128, D], BF16, kind="ExternalInput")
        wt8_h = ap("wt8", [8, 128, D], FP8, kind="ExternalInput")
        tab_h = ap("tab", [8, 128, 2048], BF16, kind="ExternalInput")
        dbc_h = ap("dbc", [128, D], F32, kind="ExternalInput")
        bias_h = ap("bias", [1, D], F32, kind="ExternalInput")
        ones_h = ap("ones_r", [1, 128], BF16, kind="ExternalInput")
        carr_h = ap("carr", [128, 64], F32, kind="ExternalInput")
        selm_h = ap("selm", [128, 256], BF16, kind="ExternalInput")
        ws_h = ap("ws_col", [128, 1], F32, kind="ExternalInput")
        y_o = ap("y", [S, D], F32, kind="ExternalOutput")

    with ExitStack() as stack:
        en = stack.enter_context
        sb = lambda name, shape, dt: en(nc.sbuf_tensor("sb_" + name, shape, dt))

        a_rep = sb("a_rep", [128, 512], F32)
        omtab = sb("omtab", [128, 2048], F32)
        ident = sb("ident", [128, 128], BF16)
        sc = {nm: sb("s_" + nm, [128, 4], F32) for nm in
              ["om", "lr", "li", "gr", "gi", "a", "b", "c", "d", "e", "f"]}
        sc_fill = sb("sc_fill", [128, 1], F32)
        pw_r = sb("pw_r", [128, 1], F32)
        pw_i = sb("pw_i", [128, 1], F32)
        PrP = sb("PrP", [128, 32], F32)
        PiP = sb("PiP", [128, 32], F32)
        ct = [sb(f"ct{m}", [128, 2048], BF16) for m in range(4)]
        stb = [sb(f"st{m}", [128, 2048], BF16) for m in range(4)]
        tmpa = sb("tmpa", [128, 512], F32)
        tmpb = sb("tmpb", [128, 512], F32)

        u_roll = [sb(f"u{i}", [128, D], F32) for i in range(4)]
        nxb = 2 if mode == "B" else 1
        xn_blk = [[sb(f"xn{b}_{i}", [128, D], BF16) for i in range(4)] for b in range(nxb)]
        xnT = [sb("xnT0", [128, 4096], BF16)]
        vbuf = [sb(f"vb{g}", [128, BT], BF16) for g in range(8)]
        vt = [sb(f"vt{g}", [128, BT], BF16) for g in range(8)]
        tb1 = sb("tb1", [128, BT], BF16)
        tb2 = sb("tb2", [128, BT], BF16)
        s0 = [sb(f"s0_{g}", [128, BT], BF16) for g in range(8)]
        zeros_bf = sb("zeros_bf", [128, BT], BF16)
        init_col = sb("init_col", [128, 8], F32)
        st_sum = sb("st_sum", [128, NT], F32)
        st_ssq = sb("st_ssq", [128, NT], F32)
        st_den = sb("st_den", [128, NT], F32)
        st_r0 = sb("st_r0", [128, NT], F32)
        st_y0 = sb("st_y0", [128, NT], F32)
        st_rstd = sb("st_rstd", [128, NT], F32)
        st_negmr = sb("st_negmr", [128, NT], F32)
        st_mean = sb("st_mean", [128, NT], F32)
        st_t1 = sb("st_t1", [128, NT], F32)
        st_t2 = sb("st_t2", [128, NT], F32)
        sq_dump = sb("sq_dump", [128, D], BF16)
        bht_sb = [sb(f"bht{d}", [128, NSL], BF16) for d in range(8)]

        if mode == "A":
            bw_sb = [sb(f"bw{q}", [128, D], BF16) for q in range(8)]
            bw_f = sb("bw_f", [128, D], F32)
            mbt_sb = [sb(f"mbt{q}", [128, NSL], BF16) for q in range(8)]
            cwt_sb = [sb(f"cwt{k}", [128, D], BF16) for k in range(8)]
            cwt_f = sb("cwt_f", [128, D], F32)
            mbts_sb = [sb(f"mbts{q}", [128, 128], BF16) for q in range(8)]
            mgts_sb = [sb(f"mgts{k}", [128, 128], BF16) for k in range(8)]
            wtb_sb = sb("wtb", [128, D], BF16)
            bsl_sb = sb("bsl", [128, NSL], BF16)
            gsl_sb = sb("gsl", [128, D], BF16)
            wsl_sb = sb("wsl", [128, D], FP8)
            carr_sb = sb("carr_sb", [128, 8], F32)
            ctf = sb("ctf", [128, 2048], F32)
            stf = sb("stf", [128, 2048], F32)
        else:
            gt_sb = [sb(f"gt{g}", [128, D], BF16) for g in range(8)]
            wt8_sb = [sb(f"wt8{d}", [128, D], FP8) for d in range(8)]
            dbc_sb = sb("dbc", [128, D], F32)
            bias_f = sb("bias_f", [1, D], F32)
            bias_b = sb("bias_b", [1, D], BF16)
            ones_sb = sb("ones_sb", [1, 128], BF16)
            carr_sb = sb("carr_sb", [128, 64], F32)
            selm_sb = sb("selm", [128, 256], BF16)
            ws_col = sb("ws_col", [128, 1], F32)
            qx_roll = [sb(f"qx{i}", [128, D], FP8) for i in range(2)]
            qxT = sb("qxT", [128, 4096], FP8)
            HT = [[sb(f"HT0_{g}", [128, BT], BF16) for g in range(8)]]
            fbuf = [sb(f"fb{i}", [128, D], F32) for i in range(4)]
            g1b = [sb("g1_0", [128, D], F32)]
            osb = [sb("os0", [128, D], F32)]
            st_amax = sb("st_amax", [128, NT], F32)
            st_qsc = sb("st_qsc", [128, NT], F32)
            st_negq = sb("st_negq", [128, NT], F32)
            st_sff = sb("st_sff", [128, NT], F32)
            c4 = {nm: sb("c_" + nm, [128, 4], F32) for nm in
                  ["mjr", "mji", "ta", "tb", "ar0", "ai0", "ar1", "ai1"]}
            t32a = sb("t32a", [128, 32], F32)
            t32b = sb("t32b", [128, 32], F32)

        ps_v = [en(nc.psum_tensor(f"psv{i}", [128, BT], F32)) for i in range(4)]
        ps_tx = en(nc.psum_tensor("ps_tx", [128, BT], F32))
        ps_tq = en(nc.psum_tensor("ps_tq", [128, BT], F32))
        ps_f = en(nc.psum_tensor("ps_f", [128, BT], F32))
        ps_m = en(nc.psum_tensor("ps_m", [128, BT], F32))

        sem = lambda nm: Sem(stack, nc, nm)
        s_w = sem("s_w"); s_u = sem("s_u"); s_st1 = sem("s_st1")
        s_stA = sem("s_stA"); s_stB = sem("s_stB"); s_stats = sem("s_stats")
        s_xn = sem("s_xn"); s_qs = sem("s_qs"); s_qx = sem("s_qx")
        s_trT = sem("s_trT"); s_trE = sem("s_trE")
        s_tqT = sem("s_tqT"); s_tqE = sem("s_tqE")
        s_vmm = sem("s_vmm"); s_vev = sem("s_vev"); s_pre = sem("s_pre")
        s_scan = sem("s_scan"); s_H = sem("s_H")
        s_mix = sem("s_mix"); s_o1 = sem("s_o1")
        s_ffn = sem("s_ffn"); s_fev = sem("s_fev"); s_ua = sem("s_ua")
        s_gp = sem("s_gp"); s_o2 = sem("s_o2"); s_set = sem("s_set"); s_sets = sem("s_sets")
        s_prep = sem("s_prep"); s_pev = sem("s_pev"); s_sv = sem("s_sv")
        s_stx = sem("s_stx")

        blk = en(nc.Block())

        @blk.sync
        def _(sy):
            def ld(dst, src):
                sy.dma_start(out=dst, in_=src).then_inc(s_w.h, 16)
                s_w.n += 16
            ld(a_rep[:], a_rep_h); ld(omtab[:], omtab_h); ld(ident[:], ident_h)
            if mode == "A":
                for q in range(8):
                    ld(mbt_sb[q][:], mbt_h[128 * q:128 * (q + 1), :])
                    ld(mbts_sb[q][:], mbts_h[128 * q:128 * (q + 1), :])
                    ld(mgts_sb[q][:], mgts_h[128 * q:128 * (q + 1), :])
                ld(wtb_sb[:], wtb_h)
            else:
                for d_ in range(8):
                    ld(bht_sb[d_][:], bht_h[d_])
                    ld(gt_sb[d_][:], gt_h[d_])
                    ld(wt8_sb[d_][:], wt8_h[d_])
                for m in range(4):
                    ld(ct[m][:], tab_h[m])
                    ld(stb[m][:], tab_h[4 + m])
                ld(dbc_sb[:], dbc_h); ld(bias_f[:], bias_h); ld(ones_sb[:], ones_h)
                ld(carr_sb[:], carr_h); ld(selm_sb[:], selm_h); ld(ws_col[:], ws_h)
            for tt in range(NT):
                if tt >= 4:
                    sy.wait_ge(s_xn.h, tt - 3)
                sy.dma_start(out=u_roll[tt % 4][:], in_=u_h[128 * tt:128 * (tt + 1), :]).then_inc(s_u.h, 16)
                s_u.n += 16
            if mode == "B":
                sy.wait_ge(s_stx.h, 16 * NT)
            else:
                sy.wait_ge(s_stx.h, 16 * 12)
        n_w = s_w.n

        if mode == "A":
            @blk.gpsimd
            def _(g):
                for q in range(8):
                    if q >= 1:
                        g.wait_ge(s_sets.h, q)
                    g.dma_start(out=bw_f[:], in_=bw_h[128 * q:128 * (q + 1), :]).then_inc(s_gp.h, 16)
                    s_gp.n += 16
                for k in range(8):
                    g.wait_ge(s_sets.h, 8 + k)
                    g.dma_start(out=cwt_f[:], in_=cwt_h[128 * k:128 * (k + 1), :]).then_inc(s_gp.h, 16)
                    s_gp.n += 16

            @blk.scalar
            def _(s):
                for q in range(8):
                    s.wait_ge(s_gp.h, 16 * (q + 1))
                    s_sets.inc(s.copy(out=bw_sb[q][:], in_=bw_f[:]))
                for k in range(8):
                    s.wait_ge(s_gp.h, 128 + 16 * (k + 1))
                    s_sets.inc(s.copy(out=cwt_sb[k][:], in_=cwt_f[:]))
                s.wait_ge(s_w.h, n_w)
                s_sets.inc(s.mul(out=wsl_sb[:], in_=wtb_sb[:], mul=0.5))

            @blk.tensor
            def _(te):
                te.wait_ge(s_sets.h, 16)
                te.wait_ge(s_w.h, n_w)
                for idx in range(16):
                    d_, h = idx // 2, idx % 2
                    if idx >= 1:
                        te.wait_ge(s_pev.h, idx)
                    for q in range(8):
                        mi = te.matmul(ps_m[:], bw_sb[q][:, 128 * d_:128 * (d_ + 1)],
                                       mbt_sb[q][:, BT * h:BT * (h + 1)],
                                       start=(q == 0), stop=(q == 7))
                    te.drain()
                    te.sem_inc(s_prep.h, 1)
                    s_prep.n += 1
                for d_ in range(8):
                    if d_ >= 1:
                        te.wait_ge(s_pev.h, 16 + d_)
                    for q in range(8):
                        mi = te.matmul(ps_tx[:, 0:128], bw_sb[q][:, 128 * d_:128 * (d_ + 1)],
                                       mbts_sb[q][:], start=(q == 0), stop=(q == 7))
                    te.drain()
                    te.sem_inc(s_prep.h, 1)
                    s_prep.n += 1
                for h in range(2):
                    if h >= 1:
                        te.wait_ge(s_pev.h, 24 + h)
                    for k in range(8):
                        mi = te.matmul(ps_tq[:], mgts_sb[k][:], cwt_sb[k][:, BT * h:BT * (h + 1)],
                                       start=(k == 0), stop=(k == 7))
                    te.drain()
                    te.sem_inc(s_prep.h, 1)
                    s_prep.n += 1

            @blk.scalar
            def _(s):
                for idx in range(16):
                    d_, h = idx // 2, idx % 2
                    s.wait_ge(s_prep.h, idx + 1)
                    s_pev.inc(s.copy(out=bht_sb[d_][:, BT * h:BT * (h + 1)], in_=ps_m[:]))
                for d_ in range(8):
                    s.wait_ge(s_prep.h, 17 + d_)
                    s_pev.inc(s.copy(out=bsl_sb[:, 128 * d_:128 * (d_ + 1)], in_=ps_tx[:, 0:128]))
                for h in range(2):
                    s.wait_ge(s_prep.h, 25 + h)
                    s_pev.inc(s.copy(out=gsl_sb[:, BT * h:BT * (h + 1)], in_=ps_tq[:]))

        @blk.vector
        def _(v):
            class VW:
                last = None

                def __getattr__(self, name):
                    f = getattr(v, name)

                    def wrap(*a, **k):
                        i = f(*a, **k)
                        VW.last = i
                        return i
                    return wrap

                def sync(self):
                    if VW.last is not None:
                        VW.last.then_inc(s_sv.h, 1)
                        s_sv.n += 1
                        v.wait_ge(s_sv.h, s_sv.n)
                        VW.last = None
            w = VW()

            def cmul4(or_, oi_, ar, ai, br, bi):
                w.tensor_tensor(out=sc["a"][:], in0=ar, in1=br, op=ALU.mult)
                w.tensor_tensor(out=sc["b"][:], in0=ai, in1=bi, op=ALU.mult)
                w.tensor_tensor(out=sc["c"][:], in0=ar, in1=bi, op=ALU.mult)
                w.tensor_tensor(out=sc["f"][:], in0=ai, in1=br, op=ALU.mult)
                w.sync()
                w.tensor_tensor(out=or_, in0=sc["a"][:], in1=sc["b"][:], op=ALU.subtract)
                w.tensor_tensor(out=oi_, in0=sc["c"][:], in1=sc["f"][:], op=ALU.add)
                w.sync()

            def renorm4(ar, ai):
                w.tensor_tensor(out=sc["a"][:], in0=ar, in1=ar, op=ALU.mult)
                w.tensor_tensor(out=sc["b"][:], in0=ai, in1=ai, op=ALU.mult)
                w.sync()
                w.tensor_tensor(out=sc["c"][:], in0=sc["a"][:], in1=sc["b"][:], op=ALU.add)
                w.sync()
                w.tensor_scalar(out=sc["c"][:], in0=sc["c"][:], scalar1=3.0, scalar2=-0.5,
                                op0=ALU.subtract, op1=ALU.mult)
                w.sync()
                w.tensor_tensor(out=sc["a"][:], in0=ar, in1=sc["c"][:], op=ALU.mult)
                w.tensor_tensor(out=sc["b"][:], in0=ai, in1=sc["c"][:], op=ALU.mult)
                w.sync()
                w.tensor_copy(out=ar, in_=sc["a"][:])
                w.tensor_copy(out=ai, in_=sc["b"][:])
                w.sync()

            v.wait_ge(s_w.h, n_w)
            w.memset(zeros_bf[:], 0.0)
            for m in range(4):
                w.tensor_tensor(out=tmpa[:], in0=omtab[:, 512 * m:512 * (m + 1)],
                                in1=a_rep[:], op=ALU.mult)
                w.sync()
                w.tensor_reduce(out=sc["om"][:, m:m + 1], in_=tmpa[:], axis=AX.X, op=ALU.add)
            w.sync()
            w.tensor_tensor(out=sc["a"][:], in0=sc["om"][:], in1=sc["om"][:], op=ALU.mult)
            w.sync()
            w.tensor_scalar(out=sc["b"][:], in0=sc["a"][:], scalar1=1.0, scalar2=None, op0=ALU.add)
            w.sync()
            w.reciprocal(out=sc["c"][:], in_=sc["b"][:])
            w.sync()
            w.tensor_tensor(out=sc["d"][:], in0=sc["b"][:], in1=sc["c"][:], op=ALU.mult)
            w.sync()
            w.tensor_scalar(out=sc["d"][:], in0=sc["d"][:], scalar1=2.0, scalar2=-1.0,
                            op0=ALU.subtract, op1=ALU.mult)
            w.sync()
            w.tensor_tensor(out=sc["e"][:], in0=sc["c"][:], in1=sc["d"][:], op=ALU.mult)
            w.tensor_scalar(out=sc["d"][:], in0=sc["a"][:], scalar1=1.0, scalar2=-1.0,
                            op0=ALU.subtract, op1=ALU.mult)
            w.sync()
            w.tensor_tensor(out=sc["lr"][:], in0=sc["d"][:], in1=sc["e"][:], op=ALU.mult)
            w.tensor_scalar(out=sc["d"][:], in0=sc["om"][:], scalar1=-2.0, scalar2=None, op0=ALU.mult)
            w.sync()
            w.tensor_tensor(out=sc["li"][:], in0=sc["d"][:], in1=sc["e"][:], op=ALU.mult)
            w.sync()

            if mode == "A":
                for m in range(4):
                    w.memset(ctf[:, 0:1], 1.0)
                    w.memset(stf[:, 0:1], 0.0)
                    w.tensor_copy(out=pw_r[:], in_=sc["lr"][:, m:m + 1])
                    w.tensor_copy(out=pw_i[:], in_=sc["li"][:, m:m + 1])
                    w.sync()
                    wlen = 1
                    while wlen < 2048:
                        subs = [(0, wlen)] if wlen <= 512 else [(j0, 512) for j0 in range(0, wlen, 512)]
                        for (j0, wsub) in subs:
                            w.tensor_scalar(out=tmpa[:, 0:wsub], in0=ctf[:, j0:j0 + wsub],
                                            scalar1=pw_r[:], scalar2=None, op0=ALU.mult)
                            w.tensor_scalar(out=tmpb[:, 0:wsub], in0=stf[:, j0:j0 + wsub],
                                            scalar1=pw_i[:], scalar2=None, op0=ALU.mult)
                            w.sync()
                            w.tensor_tensor(out=ctf[:, wlen + j0:wlen + j0 + wsub],
                                            in0=tmpa[:, 0:wsub], in1=tmpb[:, 0:wsub], op=ALU.subtract)
                            w.tensor_scalar(out=tmpa[:, 0:wsub], in0=ctf[:, j0:j0 + wsub],
                                            scalar1=pw_i[:], scalar2=None, op0=ALU.mult)
                            w.tensor_scalar(out=tmpb[:, 0:wsub], in0=stf[:, j0:j0 + wsub],
                                            scalar1=pw_r[:], scalar2=None, op0=ALU.mult)
                            w.sync()
                            w.tensor_tensor(out=stf[:, wlen + j0:wlen + j0 + wsub],
                                            in0=tmpa[:, 0:wsub], in1=tmpb[:, 0:wsub], op=ALU.add)
                        if wlen < 1024:
                            w.tensor_tensor(out=sc["a"][:, 0:1], in0=pw_r[:], in1=pw_r[:], op=ALU.mult)
                            w.tensor_tensor(out=sc["b"][:, 0:1], in0=pw_i[:], in1=pw_i[:], op=ALU.mult)
                            w.tensor_tensor(out=sc["c"][:, 0:1], in0=pw_r[:], in1=pw_i[:], op=ALU.mult)
                            w.sync()
                            w.tensor_tensor(out=pw_r[:], in0=sc["a"][:, 0:1], in1=sc["b"][:, 0:1], op=ALU.subtract)
                            w.tensor_scalar(out=pw_i[:], in0=sc["c"][:, 0:1], scalar1=2.0,
                                            scalar2=None, op0=ALU.mult)
                            w.sync()
                            w.tensor_tensor(out=sc["a"][:, 0:1], in0=pw_r[:], in1=pw_r[:], op=ALU.mult)
                            w.tensor_tensor(out=sc["b"][:, 0:1], in0=pw_i[:], in1=pw_i[:], op=ALU.mult)
                            w.sync()
                            w.tensor_tensor(out=sc["c"][:, 0:1], in0=sc["a"][:, 0:1], in1=sc["b"][:, 0:1], op=ALU.add)
                            w.sync()
                            w.tensor_scalar(out=sc["c"][:, 0:1], in0=sc["c"][:, 0:1], scalar1=3.0,
                                            scalar2=-0.5, op0=ALU.subtract, op1=ALU.mult)
                            w.sync()
                            w.tensor_tensor(out=sc["a"][:, 0:1], in0=pw_r[:], in1=sc["c"][:, 0:1], op=ALU.mult)
                            w.tensor_tensor(out=sc["b"][:, 0:1], in0=pw_i[:], in1=sc["c"][:, 0:1], op=ALU.mult)
                            w.sync()
                            w.tensor_copy(out=pw_r[:], in_=sc["a"][:, 0:1])
                            w.tensor_copy(out=pw_i[:], in_=sc["b"][:, 0:1])
                            w.sync()
                        wlen *= 2
                    w.sync()
                    w.tensor_copy(out=ct[m][:], in_=ctf[:])
                    w.tensor_copy(out=stb[m][:], in_=stf[:])
                v.drain()
                v.sem_inc(s_set.h, 1); s_set.n += 1

            w.tensor_copy(out=sc["gr"][:], in_=sc["lr"][:])
            w.tensor_copy(out=sc["gi"][:], in_=sc["li"][:])
            w.sync()
            for _ in range(11):
                cmul4(sc["d"][:], sc["e"][:], sc["gr"][:], sc["gi"][:], sc["gr"][:], sc["gi"][:])
                w.tensor_copy(out=sc["gr"][:], in_=sc["d"][:])
                w.tensor_copy(out=sc["gi"][:], in_=sc["e"][:])
                w.sync()
                renorm4(sc["gr"][:], sc["gi"][:])
            w.memset(PrP[:, 0:4], 1.0)
            w.memset(PiP[:, 0:4], 0.0)
            w.sync()
            for e_ in range(1, 8):
                cmul4(sc["d"][:], sc["e"][:], PrP[:, 4 * (e_ - 1):4 * e_], PiP[:, 4 * (e_ - 1):4 * e_],
                      sc["gr"][:], sc["gi"][:])
                w.tensor_copy(out=PrP[:, 4 * e_:4 * e_ + 4], in_=sc["d"][:])
                w.tensor_copy(out=PiP[:, 4 * e_:4 * e_ + 4], in_=sc["e"][:])
                w.sync()

            if mode == "B":
                for j in range(NCORE):
                    sel = selm_sb[:, 32 * j:32 * (j + 1)]
                    w.tensor_tensor(out=t32a[:], in0=PrP[:], in1=sel, op=ALU.mult)
                    w.tensor_tensor(out=t32b[:], in0=PiP[:], in1=sel, op=ALU.mult)
                    w.sync()
                    va = t32a[:, :].rearrange("p (e c) -> p c e", c=4)
                    vb = t32b[:, :].rearrange("p (e c) -> p c e", c=4)
                    w.tensor_reduce(out=c4["mjr"][:], in_=va, axis=AX.X, op=ALU.add)
                    w.tensor_reduce(out=c4["mji"][:], in_=vb, axis=AX.X, op=ALU.add)
                    rr = carr_sb[:, 8 * j:8 * j + 4]
                    ri = carr_sb[:, 8 * j + 4:8 * j + 8]
                    w.sync()
                    w.tensor_tensor(out=sc["a"][:], in0=c4["mjr"][:], in1=rr, op=ALU.mult)
                    w.tensor_tensor(out=sc["b"][:], in0=c4["mji"][:], in1=ri, op=ALU.mult)
                    w.tensor_tensor(out=sc["c"][:], in0=c4["mjr"][:], in1=ri, op=ALU.mult)
                    w.tensor_tensor(out=sc["f"][:], in0=c4["mji"][:], in1=rr, op=ALU.mult)
                    w.sync()
                    w.tensor_tensor(out=c4["ta"][:], in0=sc["a"][:], in1=sc["b"][:], op=ALU.subtract)
                    w.tensor_tensor(out=c4["tb"][:], in0=sc["c"][:], in1=sc["f"][:], op=ALU.add)
                    w.sync()
                    src_r = c4["ar0"] if j % 2 == 1 else c4["ar1"]
                    dst_r = c4["ar1"] if j % 2 == 1 else c4["ar0"]
                    src_i = c4["ai0"] if j % 2 == 1 else c4["ai1"]
                    dst_i = c4["ai1"] if j % 2 == 1 else c4["ai0"]
                    if j == 0:
                        w.tensor_copy(out=dst_r[:], in_=c4["ta"][:])
                        w.tensor_copy(out=dst_i[:], in_=c4["tb"][:])
                    else:
                        w.tensor_tensor(out=dst_r[:], in0=src_r[:], in1=c4["ta"][:], op=ALU.add)
                        w.tensor_tensor(out=dst_i[:], in0=src_i[:], in1=c4["tb"][:], op=ALU.add)
                    w.sync()
                fr = c4["ar1"] if (NCORE - 1) % 2 == 1 else c4["ar0"]
                fi = c4["ai1"] if (NCORE - 1) % 2 == 1 else c4["ai0"]
                w.tensor_copy(out=init_col[:, 0:4], in_=fr[:])
                w.tensor_copy(out=init_col[:, 4:8], in_=fi[:])
                v.drain()
                v.sem_inc(s_set.h, 1); s_set.n += 1

        if mode == "B":
            @blk.scalar
            def _(s):
                s.wait_ge(s_w.h, n_w)
                s_sets.inc(s.copy(out=bias_b[:], in_=bias_f[:]))

        nplane = 8
        for b in range(NB):
            bb = b % 2 if mode == "B" else 0
            tiles = [4 * b + i for i in range(4)]

            c4s = lambda arr: arr[:, 4 * b:4 * b + 4]

            @blk.vector
            def _(v, b=b, tiles=tiles):
                w = _vw(v, s_sv)
                for tau, tg in enumerate(tiles):
                    ub = u_roll[tg % 4]
                    v.wait_ge(s_u.h, 16 * (tg + 1))
                    w.tensor_tensor(out=sq_dump[:], in0=ub[:], in1=ub[:], op=ALU.mult)
                    w.tensor_reduce(out=st_sum[:, tg:tg + 1], in_=ub[:], axis=AX.X, op=ALU.add)
                    w.sync()
                    w.tensor_reduce(out=st_ssq[:, tg:tg + 1], in_=sq_dump[:], axis=AX.X, op=ALU.add)
                w.sync()
                C = lambda arr: arr[:, 4 * b:4 * b + 4]
                w.tensor_scalar(out=C(st_mean), in0=C(st_sum), scalar1=1.0 / D,
                                scalar2=None, op0=ALU.mult)
                w.tensor_scalar(out=C(st_t1), in0=C(st_ssq), scalar1=1.0 / D,
                                scalar2=None, op0=ALU.mult)
                w.sync()
                w.tensor_tensor(out=C(st_t2), in0=C(st_mean), in1=C(st_mean), op=ALU.mult)
                w.sync()
                w.scalar_tensor_tensor(out=C(st_den), in0=C(st_t2), scalar=-1.0,
                                       in1=C(st_t1), op0=ALU.mult, op1=ALU.add)
                w.sync()
                w.tensor_scalar(out=C(st_den), in0=C(st_den), scalar1=1e-5,
                                scalar2=None, op0=ALU.add)
                w.sync()
                w.tensor_scalar(out=C(st_rstd), in0=C(st_den), scalar1=3.0, scalar2=-0.5,
                                op0=ALU.subtract, op1=ALU.mult)
                w.sync()
                for _ in range(4):
                    w.tensor_tensor(out=C(st_t1), in0=C(st_rstd), in1=C(st_rstd), op=ALU.mult)
                    w.sync()
                    w.tensor_tensor(out=C(st_t2), in0=C(st_t1), in1=C(st_den), op=ALU.mult)
                    w.sync()
                    w.tensor_scalar(out=C(st_t2), in0=C(st_t2), scalar1=3.0, scalar2=-0.5,
                                    op0=ALU.subtract, op1=ALU.mult)
                    w.sync()
                    w.tensor_tensor(out=C(st_t1), in0=C(st_rstd), in1=C(st_t2), op=ALU.mult)
                    w.sync()
                    w.tensor_copy(out=C(st_rstd), in_=C(st_t1))
                    w.sync()
                w.tensor_tensor(out=C(st_t1), in0=C(st_mean), in1=C(st_rstd), op=ALU.mult)
                w.sync()
                w.tensor_scalar(out=C(st_negmr), in0=C(st_t1), scalar1=-1.0,
                                scalar2=None, op0=ALU.mult)
                w.sync()
                v.sem_inc(s_stats.h, 1); s_stats.n += 1

            @blk.scalar
            def _(s, b=b, tiles=tiles):
                s.wait_ge(s_stats.h, b + 1)
                for tau, tg in enumerate(tiles):
                    ub = u_roll[tg % 4]
                    s_xn.inc(s.activation(out=xn_blk[bb][tau][:], in_=ub[:], func=AF.Identity,
                                          scale=st_rstd[:, tg:tg + 1], bias=st_negmr[:, tg:tg + 1]))

            if mode == "B":
                @blk.vector
                def _(v, b=b, tiles=tiles):
                    v.wait_ge(s_xn.h, 4 * b + 4)
                    w = _vw(v, s_sv)
                    for tau, tg in enumerate(tiles):
                        w.tensor_reduce(out=st_amax[:, tg:tg + 1], in_=xn_blk[bb][tau][:],
                                        axis=AX.X, op=ALU.max, apply_absolute_value=True)
                    w.sync()
                    C = lambda arr: arr[:, 4 * b:4 * b + 4]
                    w.tensor_scalar(out=C(st_amax), in0=C(st_amax),
                                    scalar1=1e-12, scalar2=None, op0=ALU.max)
                    w.sync()
                    w.tensor_scalar(out=C(st_sff), in0=C(st_amax), scalar1=ws_col[:],
                                    scalar2=1.0 / 112.0, op0=ALU.mult, op1=ALU.mult)
                    w.reciprocal(out=C(st_t1), in_=C(st_amax))
                    w.sync()
                    w.tensor_scalar(out=C(st_qsc), in0=C(st_t1), scalar1=224.0,
                                    scalar2=None, op0=ALU.mult)
                    w.sync()
                    v.sem_inc(s_qs.h, 1); s_qs.n += 1

            for tau, tg in enumerate(tiles):
                @blk.tensor
                def _(te, tg=tg, bb=bb, tau=tau):
                    for grp in range(2):
                        gi = 2 * tg + grp
                        te.wait_ge(s_xn.h, tg + 1)
                        if mode == "A" and gi == 0:
                            te.wait_ge(s_pev.h, 26)
                        if gi >= 1:
                            te.wait_ge(s_trE.h, gi)
                        for k in range(4):
                            dd = 4 * grp + k
                            mi = te.matmul(ps_tx[:, 128 * k:128 * (k + 1)],
                                           xn_blk[bb][tau][:, 128 * dd:128 * (dd + 1)],
                                           ident[:], start=True, stop=True)
                        te.drain()
                        te.sem_inc(s_trT.h, 1)
                        s_trT.n += 1
                    if mode == "B":
                        for grp in range(2):
                            gi = 2 * tg + grp
                            te.wait_ge(s_qx.h, tg + 1)
                            if gi >= 1:
                                te.wait_ge(s_tqE.h, gi)
                            if tg >= 4:
                                te.wait_ge(s_ffn.h, 8 * (tg // 4))
                            for k in range(4):
                                dd = 4 * grp + k
                                mi = te.matmul(ps_tq[:, 128 * k:128 * (k + 1)],
                                               qx_roll[tg % 2][:, 128 * dd:128 * (dd + 1)],
                                               ident[:], start=True, stop=True)
                            te.drain()
                            te.sem_inc(s_tqT.h, 1)
                            s_tqT.n += 1

                @blk.scalar
                def _(s, tg=tg, b=b, bb=bb, tau=tau):
                    if mode == "B":
                        s.wait_ge(s_qs.h, b + 1)
                        if tg >= 2:
                            s.wait_ge(s_tqT.h, 2 * (tg - 1))
                        s_qx.inc(s.activation(out=qx_roll[tg % 2][:], in_=xn_blk[bb][tau][:],
                                              func=AF.Identity, scale=st_qsc[:, tg:tg + 1], bias=0.0))
                    for grp in range(2):
                        gi = 2 * tg + grp
                        s.wait_ge(s_trT.h, gi + 1)
                        if tg % 4 == 0 and tg >= 4 and grp == 0:
                            s.wait_ge(s_vmm.h, 4 * (tg // 4))
                        dst = xnT[0][:, :].rearrange("p (d c) -> p d c", d=8)
                        dst = dst[:, 4 * grp:4 * grp + 4, 128 * tau:128 * (tau + 1)]
                        s_trE.inc(s.copy(out=dst, in_=ps_tx[:, :].rearrange("p (d c) -> p d c", d=4)))
                    if mode == "B":
                        for grp in range(2):
                            gi = 2 * tg + grp
                            s.wait_ge(s_tqT.h, gi + 1)
                            dst = qxT[:, :].rearrange("p (d c) -> p d c", d=8)
                            dst = dst[:, 4 * grp:4 * grp + 4, 128 * tau:128 * (tau + 1)]
                            s_tqE.inc(s.copy(out=dst, in_=ps_tq[:, :].rearrange("p (d c) -> p d c", d=4)))

            if mode == "B":
                def emit_mix(bm):
                    @blk.tensor
                    def _(te, bm=bm):
                        te.wait_ge(s_H.h, bm + 1)
                        for tau in range(4):
                            for h in range(2):
                                idx = 8 * bm + 2 * tau + h
                                if idx >= 1:
                                    te.wait_ge(s_o1.h, idx)
                                for g_ in range(8):
                                    te.matmul(ps_m[:], HT[0][g_][:, 128 * tau:128 * (tau + 1)],
                                              gt_sb[g_][:, BT * h:BT * (h + 1)],
                                              start=(g_ == 0), stop=False)
                                mi = te.matmul(ps_m[:], ones_sb[:], bias_b[:, BT * h:BT * (h + 1)],
                                               start=False, stop=True)
                                te.drain()
                                te.sem_inc(s_mix.h, 1)
                                s_mix.n += 1

                    @blk.vector
                    def _(v, bm=bm):
                        for tau in range(4):
                            tg = 4 * bm + tau
                            for h in range(2):
                                idx = 8 * bm + 2 * tau + h
                                v.wait_ge(s_mix.h, idx + 1)
                                v.wait_ge(s_ua.h, 16 * (tg + 1))
                                if tg >= 1 and h == 0:
                                    v.wait_ge(s_o2.h, tg)
                                s_o1.inc(v.tensor_tensor(out=osb[0][:, BT * h:BT * (h + 1)],
                                                         in0=fbuf[tg % 4][:, BT * h:BT * (h + 1)],
                                                         in1=ps_m[:], op=ALU.add))

                    @blk.gpsimd
                    def _(g, bm=bm):
                        for tau in range(4):
                            tg = 4 * bm + tau
                            g.wait_ge(s_xn.h, tg + 1)
                            g.tensor_tensor(out=g1b[0][:], in0=xn_blk[bm % 2][tau][:],
                                            in1=dbc_sb[:], op=ALU.mult)
                            g.wait_ge(s_fev.h, 2 * tg + 2)
                            g.dma_start(out=fbuf[tg % 4][:], in_=u_h[128 * tg:128 * (tg + 1), :],
                                        accum_op=ALU.add).then_inc(s_ua.h, 16)
                            s_ua.n += 16
                            g.wait_ge(s_o1.h, 8 * bm + 2 * tau + 2)
                            s_o2.inc(g.tensor_tensor(out=fbuf[tg % 4][:], in0=osb[0][:],
                                                     in1=g1b[0][:], op=ALU.add))
                            g.dma_start(out=y_o[128 * tg:128 * (tg + 1), :],
                                        in_=fbuf[tg % 4][:]).then_inc(s_stx.h, 16)
                            s_stx.n += 16

                if b >= 1:
                    emit_mix(b - 1)

            @blk.tensor
            def _(te, b=b, bb=bb):
                te.wait_ge(s_trE.h, 8 * (b + 1))
                if mode == "A":
                    te.wait_ge(s_pev.h, 26)
                for p in range(4):
                    gpair = 4 * b + p
                    if gpair >= 2:
                        te.wait_ge(s_vev.h, 2 * (gpair - 1))
                    for half in range(2):
                        g_ = p + 4 * half
                        bank = ps_v[2 * (p % 2) + half]
                        for d_ in range(8):
                            mi = te.matmul(bank[:], bht_sb[d_][:, 128 * g_:128 * (g_ + 1)],
                                           xnT[0][:, 512 * d_:512 * (d_ + 1)],
                                           start=(d_ == 0), stop=(d_ == 7))
                    te.drain()
                    te.sem_inc(s_vmm.h, 1)
                    s_vmm.n += 1

            @blk.scalar
            def _(s, b=b):
                for p in range(4):
                    gpair = 4 * b + p
                    s.wait_ge(s_vmm.h, gpair + 1)
                    if b >= 1:
                        s.wait_ge(s_pre.h, 4 * (b - 1) + p + 1)
                    for half in range(2):
                        g_ = p + 4 * half
                        bank = ps_v[2 * (p % 2) + half]
                        ci = s.copy(out=vbuf[g_][:], in_=bank[:])
                    s_vev.inc(ci, 2)

            @blk.vector
            def _(v, b=b):
                for pp in range(2):
                    e_, o_ = 2 * pp, 2 * pp + 1
                    v.wait_ge(s_vev.h, 2 * (4 * b + o_ + 1))
                    csE = ct[e_][:, BT * b:BT * (b + 1)]
                    snE = stb[e_][:, BT * b:BT * (b + 1)]
                    csO = ct[o_][:, BT * b:BT * (b + 1)]
                    snO = stb[o_][:, BT * b:BT * (b + 1)]
                    vrE, viE = vbuf[e_][:], vbuf[e_ + 4][:]
                    vrO, viO = vbuf[o_][:], vbuf[o_ + 4][:]
                    v.tensor_tensor(out=tb1[:], in0=csE, in1=vrE, op=ALU.mult)
                    v.tensor_tensor(out=tb2[:], in0=csO, in1=vrO, op=ALU.mult)
                    v.tensor_tensor(out=vt[e_][:], in0=snE, in1=viE, op=ALU.mult)
                    v.tensor_tensor(out=vt[o_][:], in0=snO, in1=viO, op=ALU.mult)
                    v.tensor_tensor(out=vt[e_][:], in0=tb1[:], in1=vt[e_][:], op=ALU.add)
                    v.tensor_tensor(out=vt[o_][:], in0=tb2[:], in1=vt[o_][:], op=ALU.add)
                    v.tensor_tensor(out=vt[e_ + 4][:], in0=snE, in1=vrE, op=ALU.mult)
                    v.tensor_tensor(out=vt[o_ + 4][:], in0=snO, in1=vrO, op=ALU.mult)
                    v.tensor_tensor(out=tb1[:], in0=csE, in1=viE, op=ALU.mult)
                    v.tensor_tensor(out=tb2[:], in0=csO, in1=viO, op=ALU.mult)
                    v.tensor_tensor(out=vt[e_ + 4][:], in0=tb1[:], in1=vt[e_ + 4][:], op=ALU.subtract)
                    s_pre.inc(v.tensor_tensor(out=vt[o_ + 4][:], in0=tb2[:], in1=vt[o_ + 4][:],
                                              op=ALU.subtract), 2)
                if b == 0:
                    v.wait_ge(s_set.h, 1 if mode == "B" else 1)
                for g_ in range(8):
                    if b == 0 and mode == "B":
                        init = init_col[:, g_:g_ + 1]
                    elif b == 0:
                        init = 0.0
                    else:
                        init = s0[g_][:, BT - 1:BT]
                    si = v.tensor_tensor_scan(out=s0[g_][:], data0=vt[g_][:], data1=zeros_bf[:],
                                              initial=init, op0=ALU.add, op1=ALU.add)
                s_scan.inc(si)
                if mode == "B":
                    if b >= 1:
                        v.wait_ge(s_mix.h, 8 * b)
                    for pp in range(2):
                        e_, o_ = 2 * pp, 2 * pp + 1
                        csE = ct[e_][:, BT * b:BT * (b + 1)]
                        snE = stb[e_][:, BT * b:BT * (b + 1)]
                        csO = ct[o_][:, BT * b:BT * (b + 1)]
                        snO = stb[o_][:, BT * b:BT * (b + 1)]
                        srE, siE = s0[e_][:], s0[e_ + 4][:]
                        srO, siO = s0[o_][:], s0[o_ + 4][:]
                        hrE, hiE = HT[0][e_][:], HT[0][e_ + 4][:]
                        hrO, hiO = HT[0][o_][:], HT[0][o_ + 4][:]
                        v.tensor_tensor(out=hrE, in0=snE, in1=siE, op=ALU.mult)
                        v.tensor_tensor(out=hrO, in0=snO, in1=siO, op=ALU.mult)
                        v.tensor_tensor(out=tb1[:], in0=csE, in1=srE, op=ALU.mult)
                        v.tensor_tensor(out=tb2[:], in0=csO, in1=srO, op=ALU.mult)
                        v.tensor_tensor(out=hrE, in0=tb1[:], in1=hrE, op=ALU.subtract)
                        v.tensor_tensor(out=hrO, in0=tb2[:], in1=hrO, op=ALU.subtract)
                        v.tensor_tensor(out=hiE, in0=snE, in1=srE, op=ALU.mult)
                        v.tensor_tensor(out=hiO, in0=snO, in1=srO, op=ALU.mult)
                        v.tensor_tensor(out=tb1[:], in0=csE, in1=siE, op=ALU.mult)
                        v.tensor_tensor(out=tb2[:], in0=csO, in1=siO, op=ALU.mult)
                        v.tensor_tensor(out=hiE, in0=tb1[:], in1=hiE, op=ALU.add)
                        hlast = v.tensor_tensor(out=hiO, in0=tb2[:], in1=hiO, op=ALU.add)
                    s_H.inc(hlast)

            if mode == "B":
                @blk.tensor
                def _(te, b=b):
                    for tau in range(4):
                        tg = 4 * b + tau
                        for h in range(2):
                            idx = 8 * b + 2 * tau + h
                            te.wait_ge(s_tqE.h, 2 * tg + 2)
                            if idx >= 1:
                                te.wait_ge(s_fev.h, idx)
                            for d_ in range(8):
                                mi = te.matmul(ps_f[:], qxT[:, 512 * d_ + 128 * tau:512 * d_ + 128 * (tau + 1)],
                                               wt8_sb[d_][:, BT * h:BT * (h + 1)],
                                               start=(d_ == 0), stop=(d_ == 7))
                            te.drain()
                            te.sem_inc(s_ffn.h, 1)
                            s_ffn.n += 1

                @blk.scalar
                def _(s, b=b):
                    for tau in range(4):
                        tg = 4 * b + tau
                        for h in range(2):
                            idx = 8 * b + 2 * tau + h
                            s.wait_ge(s_ffn.h, idx + 1)
                            if tg >= 4:
                                s.wait_ge(s_o2.h, tg - 3)
                                s.wait_ge(s_stx.h, 16 * (tg - 3))
                            s_fev.inc(s.activation(out=fbuf[tg % 4][:, BT * h:BT * (h + 1)],
                                                   in_=ps_f[:], func=AF.Identity,
                                                   scale=st_sff[:, tg:tg + 1], bias=0.0))

                if b == NB - 1:
                    emit_mix(b)

        if mode == "A":
            @blk.vector
            def _(v):
                v.wait_ge(s_scan.h, NB)
                for g_ in range(8):
                    v.tensor_copy(out=carr_sb[:, g_:g_ + 1], in_=s0[g_][:, BT - 1:BT])
                v.drain()
                v.sem_inc(s_set.h, 1); s_set.n += 1

            @blk.gpsimd
            def _(g):
                g.wait_ge(s_set.h, 2)
                g.wait_ge(s_pev.h, 26)
                g.wait_ge(s_sets.h, 17)
                g.dma_start(out=carr_o, in_=carr_sb[:]).then_inc(s_stx.h, 16)
                g.dma_start(out=bsl_o, in_=bsl_sb[:]).then_inc(s_stx.h, 16)
                g.dma_start(out=gsl_o, in_=gsl_sb[:]).then_inc(s_stx.h, 16)
                g.dma_start(out=wsl_o, in_=wsl_sb[:]).then_inc(s_stx.h, 16)
                for m in range(4):
                    g.dma_start(out=tab_o[m], in_=ct[m][:]).then_inc(s_stx.h, 16)
                    g.dma_start(out=tab_o[4 + m], in_=stb[m][:]).then_inc(s_stx.h, 16)
                s_stx.n += 16 * 12
                n_tap = 0
                if taps:
                    for tsrc in [st_sum, st_den, st_r0, st_y0, st_rstd, st_negmr]:
                        g.dma_start(out=tap_o[n_tap][:, 0:NT], in_=tsrc[:]).then_inc(s_stx.h, 16)
                        n_tap += 1
                    g.dma_start(out=tap_o[6][:, 0:512], in_=vbuf[0][:]).then_inc(s_stx.h, 16)
                    g.dma_start(out=tap_o[7][:, 0:512], in_=s0[0][:]).then_inc(s_stx.h, 16)
                    s_stx.n += 16 * 8
                g.wait_ge(s_stx.h, 16 * (12 + (8 if taps else 0)))

    return nc



_CONST = {}


def _consts():
    if _CONST:
        return _CONST
    N = 1024
    s_idx = np.arange(NSL)
    f_of = np.where(s_idx <= 512, s_idx, s_idx - 512)
    q = np.arange(N)
    ang = 2.0 * np.pi * np.outer(f_of, q) / N
    MB = np.where((s_idx <= 512)[:, None], np.cos(ang), -np.sin(ang))
    alpha = np.where((f_of == 0) | (f_of == 512), 1.0, 2.0)
    MG = np.where((s_idx <= 512)[:, None], np.cos(ang), -np.sin(ang)) * alpha[:, None] / N
    _CONST["MBT"] = np.ascontiguousarray(MB.T).astype(BF)
    _CONST["MGT"] = np.ascontiguousarray(MG.T).astype(BF)
    p = np.arange(128)
    om = np.zeros((128, 2048), np.float32)
    for m in range(4):
        s_pair = 128 * m + p
        jj = np.arange(512)
        val = -2.0 * np.sin(2.0 * np.pi * np.outer(s_pair, jj + 1) / N)
        val[:, 511] = 0.0
        om[:, 512 * m:512 * (m + 1)] = val
    _CONST["omtab"] = om
    _CONST["ident"] = np.eye(128).astype(BF)
    _CONST["ones_r"] = np.ones((1, 128), BF)
    _CONST["ncF"] = build_fused()
    return _CONST


def kernel(L0_attn_u, L0_attn_a_params, L0_attn_B_w, L0_attn_C_w, L0_attn_D,
           L0_ffn_weight, L0_ffn_weight_scale, L0_ffn_bias):
    C = _consts()
    u = np.ascontiguousarray(np.asarray(L0_attn_u, np.float32).reshape(16384, D))
    a_p = np.asarray(L0_attn_a_params, np.float32)
    a_rep = np.zeros((128, 512), np.float32)
    a_rep[:, :511] = a_p[None, :511]
    Bw = np.ascontiguousarray(np.asarray(L0_attn_B_w, np.float32))
    CwT = np.ascontiguousarray(np.asarray(L0_attn_C_w, np.float32).T)
    WT_bf = np.ascontiguousarray(
        np.asarray(L0_ffn_weight).astype(BF).T)
    ws = float(np.asarray(L0_ffn_weight_scale, np.float32))
    bias = np.asarray(L0_ffn_bias, np.float32).reshape(1, D)
    dbc = np.broadcast_to(np.asarray(L0_attn_D, np.float32), (128, D)).copy()
    ws_col = np.zeros((128, 4), np.float32)
    ws_col[:, 0] = ws; ws_col[:, 1] = 1.0; ws_col[:, 3] = 0.5
    uf = np.ascontiguousarray(u[:7 * S])

    in_F = []
    for i in range(NCORE):
        selm = np.zeros((128, 256), np.float32)
        for j in range(NCORE):
            e_ = i - j
            if 1 <= e_ <= 7:
                for c_ in range(4):
                    selm[:, 32 * j + 4 * e_ + c_] = 1.0
        in_F.append(dict(
            u=u[S * i:S * (i + 1)], uf=uf,
            a_rep=a_rep, omtab=C["omtab"], ident=C["ident"],
            bw=Bw, cwt=CwT, mbt=C["MBT"], mgt=C["MGT"], wtb=WT_bf,
            dbc=dbc, bias=bias, ones_r=C["ones_r"],
            selm=selm.astype(BF), ws_col=ws_col,
        ))
    res = run_bass_kernel_spmd(C["ncF"], in_F, core_ids=list(range(NCORE)))
    y = np.concatenate([np.asarray(res.results[i]["y"], np.float32) for i in range(NCORE)], axis=0)
    return y.reshape(1, 16384, D)



def build_fused():
    nc = bass.Bass("TRN2", target_bir_lowering=False, detect_race_conditions=False)
    ap = lambda *a, **k: nc.dram_tensor(*a, **k).ap()

    u_h = ap("u", [S, D], F32, kind="ExternalInput")
    uf_h = ap("uf", [7 * S, D], F32, kind="ExternalInput")
    a_rep_h = ap("a_rep", [128, 512], F32, kind="ExternalInput")
    omtab_h = ap("omtab", [128, 2048], F32, kind="ExternalInput")
    ident_h = ap("ident", [128, 128], BF16, kind="ExternalInput")
    bw_h = ap("bw", [D, D], F32, kind="ExternalInput")
    cwt_h = ap("cwt", [D, D], F32, kind="ExternalInput")
    mbt_h = ap("mbt", [D, NSL], BF16, kind="ExternalInput")
    mgt_h = ap("mgt", [D, NSL], BF16, kind="ExternalInput")
    wtb_h = ap("wtb", [D, D], BF16, kind="ExternalInput")
    dbc_h = ap("dbc", [128, D], F32, kind="ExternalInput")
    bias_h = ap("bias", [1, D], F32, kind="ExternalInput")
    ones_h = ap("ones_r", [1, 128], BF16, kind="ExternalInput")
    selm_h = ap("selm", [128, 256], BF16, kind="ExternalInput")
    ws_h = ap("ws_col", [128, 4], F32, kind="ExternalInput")
    y_o = ap("y", [S, D], F32, kind="ExternalOutput")

    NCB = 8
    NTI = 16 * NCB

    with ExitStack() as stack:
        en = stack.enter_context
        sb = lambda name, shape, dt: en(nc.sbuf_tensor("sb_" + name, shape, dt))

        ident = sb("ident", [128, 128], BF16)
        sc = {nm: sb("s_" + nm, [128, 4], F32) for nm in
              ["om", "lr", "li", "gr", "gi", "a", "b", "c", "d", "e", "f"]}
        pw_r = sb("pw_r", [128, 4], F32)
        pw_i = sb("pw_i", [128, 4], F32)
        pw_ni = sb("pw_ni", [128, 4], F32)
        PrP = sb("PrP", [128, 32], F32)
        PiP = sb("PiP", [128, 32], F32)
        ct = [sb(f"ct{m}", [128, 2048], BF16) for m in range(4)]
        stb = [sb(f"st{m}", [128, 2048], BF16) for m in range(4)]
        init_col = sb("init_col", [128, 8], F32)
        st_sum = sb("st_sum", [128, NT], F32)
        st_ssq = sb("st_ssq", [128, NT], F32)
        st_den = sb("st_den", [128, NT], F32)
        st_rstd = sb("st_rstd", [128, NT], F32)
        st_negmr = sb("st_negmr", [128, NT], F32)
        st_mean = sb("st_mean", [128, NT], F32)
        st_t1 = sb("st_t1", [128, NT], F32)
        st_t2 = sb("st_t2", [128, NT], F32)
        st_amax = sb("st_amax", [128, NT], F32)
        st_qsc = sb("st_qsc", [128, NT], F32)
        st_sff = sb("st_sff", [128, NT], F32)
        sq_dump = [sb(f"sq_dump{i}", [128, D], BF16) for i in range(4)]
        bht_sb = [sb(f"bht{d}", [128, NSL], BF16) for d in range(8)]
        gt_sb = [sb(f"gt{g}", [128, D], BF16) for g in range(8)]
        wt8_sb = [sb(f"wt8{d}", [128, D], FP8) for d in range(8)]
        dbc_sb = sb("dbc", [128, D], BF16)
        bias_b = sb("bias_b", [1, D], BF16)
        ones_sb = sb("ones_sb", [1, 128], BF16)
        carr_sb = sb("carr_sb", [128, 64], F32)
        selm_sb = sb("selm", [128, 256], BF16)
        ws_col = sb("ws_col", [128, 4], F32)
        c4 = {nm: sb("c_" + nm, [128, 4], F32) for nm in
              ["mjr", "mji", "ta", "tb", "ar0", "ai0", "ar1", "ai1"]}
        t32a = sb("t32a", [128, 32], F32)
        t32b = sb("t32b", [128, 32], F32)
        rtmp = sb("rtmp", [128, 8], F32)
        racc = [sb(f"racc{i}", [128, 8], F32) for i in range(2)]

        ph0 = ExitStack()
        sb0 = lambda name, shape, dt: ph0.enter_context(nc.sbuf_tensor("sb_" + name, shape, dt))
        a_rep = sb0("a_rep", [128, 512], F32)
        omtab = sb0("omtab", [128, 2048], F32)
        tmpa = sb0("tmpa", [128, 512], F32)
        ctf = [sb0(f"ctf{m}", [128, 2048], F32) for m in range(4)]
        stf = [sb0(f"stf{m}", [128, 2048], F32) for m in range(4)]
        wf = [sb0(f"wf{i}", [128, D], F32) for i in range(2)]
        wstage = [sb0(f"wstg{q}", [128, D], BF16) for q in range(8)]
        mstage = [sb0(f"mstg{q}", [128, NSL], BF16) for q in range(8)]

        ps_v = [en(nc.psum_tensor(f"psv{i}", [128, BT], F32)) for i in range(4)]
        ps_tx = en(nc.psum_tensor("ps_tx", [128, BT], F32))
        ps_tq = en(nc.psum_tensor("ps_tq", [128, BT], F32))
        ps_f = en(nc.psum_tensor("ps_f", [128, BT], F32))
        ps_m = en(nc.psum_tensor("ps_m", [128, BT], F32))

        sem = lambda nm: Sem(stack, nc, nm)
        s_w = sem("s_w"); s_u = sem("s_u"); s_stats = sem("s_stats")
        s_xn = sem("s_xn"); s_qs = sem("s_qs"); s_qx = sem("s_qx")
        s_trT = sem("s_trT"); s_trE = sem("s_trE")
        s_tqT = sem("s_tqT"); s_tqE = sem("s_tqE")
        s_vmm = sem("s_vmm"); s_vev = sem("s_vev"); s_pre = sem("s_pre")
        s_scan = sem("s_scan"); s_H = sem("s_H")
        s_mix = sem("s_mix"); s_o1 = sem("s_o1")
        s_ffn = sem("s_ffn"); s_fev = sem("s_fev"); s_ua = sem("s_ua")
        s_gp = sem("s_gp"); s_o2 = sem("s_o2"); s_set = sem("s_set"); s_sets = sem("s_sets")
        s_prep = sem("s_prep"); s_pev = sem("s_pev"); s_sv = sem("s_sv")
        s_gsum = sem("s_gsum"); s_gpr2 = sem("s_gpr2"); s_rdc = sem("s_rdc")
        s_stx = sem("s_stx")

        blk = en(nc.Block())

        @blk.sync
        def _(sy):
            def ld(dst, src):
                sy.dma_start(out=dst, in_=src).then_inc(s_w.h, 16)
                s_w.n += 16
            ld(a_rep[:], a_rep_h); ld(omtab[:], omtab_h)
            ld(ident[:], ident_h)
            ld(ones_sb[:], ones_h); ld(selm_sb[:], selm_h); ld(ws_col[:], ws_h)
            for q in range(8):
                ld(mstage[q][:], mbt_h[128 * q:128 * (q + 1), :])
            sy.wait_ge(s_prep.h, 16)
            for k in range(8):
                ld(mstage[k][:], mgt_h[128 * k:128 * (k + 1), :])
            sy.wait_ge(s_prep.h, 32)
            for d_ in range(8):
                if d_ >= 2:
                    sy.wait_ge(s_sets.h, 16 + d_ - 1)
                ld(mstage[d_ % 2][:], wtb_h[128 * d_:128 * (d_ + 1), :])
        n_w0 = s_w.n

        @blk.gpsimd
        def _(g):
            for q in range(8):
                if q >= 2:
                    g.wait_ge(s_sets.h, q - 1)
                g.dma_start(out=wf[q % 2][:], in_=bw_h[128 * q:128 * (q + 1), :]).then_inc(s_gp.h, 16)
                s_gp.n += 16
            for k in range(8):
                g.wait_ge(s_sets.h, 7 + k)
                g.dma_start(out=wf[k % 2][:], in_=cwt_h[128 * k:128 * (k + 1), :]).then_inc(s_gp.h, 16)
                s_gp.n += 16
            g.wait_ge(s_sets.h, 15)
            g.dma_start(out=wf[0][:], in_=dbc_h).then_inc(s_gp.h, 16)
            s_gp.n += 16
            g.wait_ge(s_sets.h, 16)
            g.dma_start(out=wf[1][0:1, :], in_=bias_h).then_inc(s_gp.h, 16)
            s_gp.n += 16

        @blk.scalar
        def _(s):
            s.wait_ge(s_w.h, 96)
            one_c, zero_c, half_c = ws_col[:, 1:2], ws_col[:, 2:3], ws_col[:, 3:4]
            for q in range(8):
                s.wait_ge(s_gp.h, 16 * (q + 1))
                s_sets.inc(s.activation(out=wstage[q][:], in_=wf[q % 2][:], func=AF.Identity, scale=one_c, bias=zero_c))
            for idx in range(16):
                d_, h = idx // 2, idx % 2
                bank = ps_m if idx % 2 == 0 else ps_tx
                s.wait_ge(s_prep.h, idx + 1)
                s_pev.inc(s.activation(out=bht_sb[d_][:, BT * h:BT * (h + 1)], in_=bank[:], func=AF.Identity, scale=one_c, bias=zero_c))
            for k in range(8):
                s.wait_ge(s_gp.h, 128 + 16 * (k + 1))
                s_sets.inc(s.activation(out=wstage[k][:], in_=wf[k % 2][:], func=AF.Identity, scale=one_c, bias=zero_c))
            for idx in range(16):
                g_, h = idx // 2, idx % 2
                bank = ps_tq if idx % 2 == 0 else ps_f
                s.wait_ge(s_prep.h, 17 + idx)
                s_pev.inc(s.activation(out=gt_sb[g_][:, BT * h:BT * (h + 1)], in_=bank[:], func=AF.Identity, scale=one_c, bias=zero_c))
            for d_ in range(8):
                s.wait_ge(s_w.h, 352 + 16 * (d_ + 1))
                s_sets.inc(s.activation(out=wt8_sb[d_][:], in_=mstage[d_ % 2][:], func=AF.Identity, scale=half_c, bias=zero_c))
            s.wait_ge(s_w.h, 128)
            s.wait_ge(s_gp.h, 16 * 17)
            s_sets.inc(s.activation(out=dbc_sb[:], in_=wf[0][:], func=AF.Identity,
                                    scale=one_c, bias=zero_c))
            s.wait_ge(s_gp.h, 16 * 18)
            s_sets.inc(s.activation(out=bias_b[:], in_=wf[1][0:1, :], func=AF.Identity,
                                    scale=ws_col[0:1, 1:2], bias=ws_col[0:1, 2:3]))

        @blk.tensor
        def _(te):
            te.wait_ge(s_sets.h, 8)
            te.wait_ge(s_w.h, 224)
            for idx in range(16):
                d_, h = idx // 2, idx % 2
                bank = ps_m if idx % 2 == 0 else ps_tx
                if idx >= 2:
                    te.wait_ge(s_pev.h, idx - 1)
                for q in range(8):
                    mi = te.matmul(bank[:], wstage[q][:, 128 * d_:128 * (d_ + 1)],
                                   mstage[q][:, BT * h:BT * (h + 1)],
                                   start=(q == 0), stop=(q == 7))
                s_prep.inc(mi)
            te.wait_ge(s_sets.h, 16)
            te.wait_ge(s_w.h, 352)
            for idx in range(16):
                g_, h = idx // 2, idx % 2
                bank = ps_tq if idx % 2 == 0 else ps_f
                if idx >= 2:
                    te.wait_ge(s_pev.h, 16 + idx - 1)
                for k in range(8):
                    mi = te.matmul(bank[:], mstage[k][:, 128 * g_:128 * (g_ + 1)],
                                   wstage[k][:, BT * h:BT * (h + 1)],
                                   start=(k == 0), stop=(k == 7))
                s_prep.inc(mi)

        @blk.vector
        def _(v):
            w = _vw(v, s_sv)

            def cmul4(or_, oi_, ar, ai, br, bi):
                w.tensor_tensor(out=sc["a"][:], in0=ar, in1=br, op=ALU.mult)
                w.tensor_tensor(out=sc["b"][:], in0=ai, in1=bi, op=ALU.mult)
                w.tensor_tensor(out=sc["c"][:], in0=ar, in1=bi, op=ALU.mult)
                w.tensor_tensor(out=sc["f"][:], in0=ai, in1=br, op=ALU.mult)
                w.sync()
                w.tensor_tensor(out=or_, in0=sc["a"][:], in1=sc["b"][:], op=ALU.subtract)
                w.tensor_tensor(out=oi_, in0=sc["c"][:], in1=sc["f"][:], op=ALU.add)
                w.sync()

            def renorm4(ar, ai):
                w.tensor_tensor(out=sc["a"][:], in0=ar, in1=ar, op=ALU.mult)
                w.tensor_tensor(out=sc["b"][:], in0=ai, in1=ai, op=ALU.mult)
                w.sync()
                w.tensor_tensor(out=sc["c"][:], in0=sc["a"][:], in1=sc["b"][:], op=ALU.add)
                w.sync()
                w.tensor_scalar(out=sc["c"][:], in0=sc["c"][:], scalar1=3.0, scalar2=-0.5,
                                op0=ALU.subtract, op1=ALU.mult)
                w.sync()
                w.tensor_tensor(out=sc["a"][:], in0=ar, in1=sc["c"][:], op=ALU.mult)
                w.tensor_tensor(out=sc["b"][:], in0=ai, in1=sc["c"][:], op=ALU.mult)
                w.sync()
                w.tensor_copy(out=ar, in_=sc["a"][:])
                w.tensor_copy(out=ai, in_=sc["b"][:])
                w.sync()

            v.wait_ge(s_w.h, 32)
            w.memset(carr_sb[:, 56:64], 0.0)
            for m in range(4):
                w.tensor_tensor(out=tmpa[:], in0=omtab[:, 512 * m:512 * (m + 1)],
                                in1=a_rep[:], op=ALU.mult)
                w.sync()
                w.tensor_reduce(out=sc["om"][:, m:m + 1], in_=tmpa[:], axis=AX.X, op=ALU.add)
            w.sync()
            w.tensor_tensor(out=sc["a"][:], in0=sc["om"][:], in1=sc["om"][:], op=ALU.mult)
            w.sync()
            w.tensor_scalar(out=sc["b"][:], in0=sc["a"][:], scalar1=1.0, scalar2=None, op0=ALU.add)
            w.sync()
            w.reciprocal(out=sc["c"][:], in_=sc["b"][:])
            w.sync()
            w.tensor_tensor(out=sc["d"][:], in0=sc["b"][:], in1=sc["c"][:], op=ALU.mult)
            w.sync()
            w.tensor_scalar(out=sc["d"][:], in0=sc["d"][:], scalar1=2.0, scalar2=-1.0,
                            op0=ALU.subtract, op1=ALU.mult)
            w.sync()
            w.tensor_tensor(out=sc["e"][:], in0=sc["c"][:], in1=sc["d"][:], op=ALU.mult)
            w.tensor_scalar(out=sc["d"][:], in0=sc["a"][:], scalar1=1.0, scalar2=-1.0,
                            op0=ALU.subtract, op1=ALU.mult)
            w.sync()
            w.tensor_tensor(out=sc["lr"][:], in0=sc["d"][:], in1=sc["e"][:], op=ALU.mult)
            w.tensor_scalar(out=sc["d"][:], in0=sc["om"][:], scalar1=-2.0, scalar2=None, op0=ALU.mult)
            w.sync()
            w.tensor_tensor(out=sc["li"][:], in0=sc["d"][:], in1=sc["e"][:], op=ALU.mult)
            w.sync()

            for m in range(4):
                w.memset(ctf[m][:, 0:1], 1.0)
                w.memset(stf[m][:, 0:1], 0.0)
            w.tensor_copy(out=pw_r[:], in_=sc["lr"][:])
            w.tensor_copy(out=pw_i[:], in_=sc["li"][:])
            w.sync()
            w.tensor_scalar(out=pw_ni[:], in0=pw_i[:], scalar1=-1.0, scalar2=None, op0=ALU.mult)
            w.sync()
            wlen = 1
            while wlen < 2048:
                subs = [(0, wlen)] if wlen <= 512 else [(j0, 512) for j0 in range(0, wlen, 512)]
                for (j0, wsub) in subs:
                    for m in range(4):
                        w.tensor_scalar(out=ctf[m][:, wlen + j0:wlen + j0 + wsub],
                                        in0=ctf[m][:, j0:j0 + wsub],
                                        scalar1=pw_r[:, m:m + 1], scalar2=None, op0=ALU.mult)
                        w.tensor_scalar(out=stf[m][:, wlen + j0:wlen + j0 + wsub],
                                        in0=stf[m][:, j0:j0 + wsub],
                                        scalar1=pw_r[:, m:m + 1], scalar2=None, op0=ALU.mult)
                    for m in range(4):
                        w.scalar_tensor_tensor(out=ctf[m][:, wlen + j0:wlen + j0 + wsub],
                                               in0=stf[m][:, j0:j0 + wsub],
                                               scalar=pw_ni[:, m:m + 1],
                                               in1=ctf[m][:, wlen + j0:wlen + j0 + wsub],
                                               op0=ALU.mult, op1=ALU.add)
                        w.scalar_tensor_tensor(out=stf[m][:, wlen + j0:wlen + j0 + wsub],
                                               in0=ctf[m][:, j0:j0 + wsub],
                                               scalar=pw_i[:, m:m + 1],
                                               in1=stf[m][:, wlen + j0:wlen + j0 + wsub],
                                               op0=ALU.mult, op1=ALU.add)
                    w.sync()
                if wlen < 1024:
                    w.tensor_tensor(out=sc["a"][:], in0=pw_r[:], in1=pw_r[:], op=ALU.mult)
                    w.tensor_tensor(out=sc["b"][:], in0=pw_i[:], in1=pw_i[:], op=ALU.mult)
                    w.tensor_tensor(out=sc["c"][:], in0=pw_r[:], in1=pw_i[:], op=ALU.mult)
                    w.sync()
                    w.tensor_tensor(out=pw_r[:], in0=sc["a"][:], in1=sc["b"][:], op=ALU.subtract)
                    w.tensor_scalar(out=pw_i[:], in0=sc["c"][:], scalar1=2.0,
                                    scalar2=None, op0=ALU.mult)
                    w.sync()
                    w.tensor_scalar(out=pw_ni[:], in0=pw_i[:], scalar1=-1.0, scalar2=None, op0=ALU.mult)
                    w.sync()
                wlen *= 2
            w.sync()
            for m in range(4):
                w.tensor_copy(out=ct[m][:], in_=ctf[m][:])
                w.tensor_copy(out=stb[m][:], in_=stf[m][:])
            v.drain()
            v.sem_inc(s_set.h, 1); s_set.n += 1

            w.tensor_copy(out=sc["gr"][:], in_=sc["lr"][:])
            w.tensor_copy(out=sc["gi"][:], in_=sc["li"][:])
            w.sync()
            for _ in range(11):
                cmul4(sc["d"][:], sc["e"][:], sc["gr"][:], sc["gi"][:], sc["gr"][:], sc["gi"][:])
                w.tensor_copy(out=sc["gr"][:], in_=sc["d"][:])
                w.tensor_copy(out=sc["gi"][:], in_=sc["e"][:])
                w.sync()
            w.memset(PrP[:, 0:4], 1.0)
            w.memset(PiP[:, 0:4], 0.0)
            w.sync()
            for e_ in range(1, 8):
                cmul4(sc["d"][:], sc["e"][:], PrP[:, 4 * (e_ - 1):4 * e_], PiP[:, 4 * (e_ - 1):4 * e_],
                      sc["gr"][:], sc["gi"][:])
                w.tensor_copy(out=PrP[:, 4 * e_:4 * e_ + 4], in_=sc["d"][:])
                w.tensor_copy(out=PiP[:, 4 * e_:4 * e_ + 4], in_=sc["e"][:])
                w.sync()

        ph0.close()
        u_roll = [sb(f"u{i}", [128, D], F32) for i in range(5)]
        xn_blk = [[sb(f"xn{b}_{i}", [128, D], BF16) for i in range(4)] for b in range(2)]
        xnT = [sb(f"xnT{i}", [128, 4096], BF16) for i in range(2)]
        qx_roll = [sb(f"qx{i}", [128, D], FP8) for i in range(2)]
        qxT = sb("qxT", [128, 4096], FP8)
        vbuf = [sb(f"vb{g}", [128, BT], BF16) for g in range(8)]
        vt = [sb(f"vt{g}", [128, BT], BF16) for g in range(8)]
        tb1 = sb("tb1", [128, BT], BF16)
        tb2 = sb("tb2", [128, BT], BF16)
        s0 = [sb(f"s0_{g}", [128, BT], BF16) for g in range(8)]
        HT = [sb(f"HT0_{g}", [128, BT], BF16) for g in range(8)]
        fbuf = [sb(f"fb{i}", [128, D], F32) for i in range(4)]
        g1b = sb("g1_0", [128, D], BF16)
        osb = [sb(f"os{i}", [128, D], F32) for i in range(2)]

        @blk.sync
        def _(sy):
            sy.wait_ge(s_pev.h, 32)
            sy.wait_ge(s_sets.h, 26)
            sy.wait_ge(s_set.h, 1)
            for tt in range(NTI):
                if tt >= 5:
                    sy.wait_ge(s_xn.h, tt - 4)
                src = uf_h[128 * tt:128 * (tt + 1), :] if tt < 112 else \
                    u_h[128 * (tt - 112):128 * (tt - 111), :]
                sy.dma_start(out=u_roll[tt % 5][:], in_=src).then_inc(s_u.h, 16)
                s_u.n += 16
            sy.wait_ge(s_stx.h, 16 * NT)

        def emit_mix(bm):
            @blk.tensor
            def _(te, bm=bm):
                te.wait_ge(s_H.h, bm + 1)
                for tau in range(4):
                    for h in range(2):
                        idx = 8 * bm + 2 * tau + h
                        if idx >= 2:
                            te.wait_ge(s_o1.h, idx - 1)
                        bank = ps_v[2 + idx % 2]
                        for g_ in range(8):
                            te.matmul(bank[:], HT[g_][:, 128 * tau:128 * (tau + 1)],
                                      gt_sb[g_][:, BT * h:BT * (h + 1)],
                                      start=(g_ == 0), stop=False)
                        mi = te.matmul(bank[:], ones_sb[:], bias_b[:, BT * h:BT * (h + 1)],
                                       start=False, stop=True)
                        s_mix.inc(mi)

            @blk.vector
            def _(v, bm=bm):
                for tau in range(4):
                    tg = 4 * bm + tau
                    for h in range(2):
                        idx = 8 * bm + 2 * tau + h
                        v.wait_ge(s_mix.h, idx + 1)
                        v.wait_ge(s_ua.h, 32 * tg + 16 * (h + 1))
                        if tg >= 2 and h == 0:
                            v.wait_ge(s_o2.h, tg - 1)
                        s_o1.inc(v.tensor_tensor(out=osb[tg % 2][:, BT * h:BT * (h + 1)],
                                                 in0=fbuf[tg % 4][:, BT * h:BT * (h + 1)],
                                                 in1=ps_v[2 + idx % 2][:], op=ALU.add))

            @blk.gpsimd
            def _(g, bm=bm):
                for tau in range(4):
                    tg = 4 * bm + tau
                    g.wait_ge(s_xn.h, 112 + tg + 1)
                    g.tensor_tensor(out=g1b[:], in0=xn_blk[bm % 2][tau][:],
                                    in1=dbc_sb[:], op=ALU.mult)
                    for h in range(2):
                        g.wait_ge(s_fev.h, 2 * tg + h + 1)
                        g.dma_start(out=fbuf[tg % 4][:, BT * h:BT * (h + 1)],
                                    in_=u_h[128 * tg:128 * (tg + 1), BT * h:BT * (h + 1)],
                                    accum_op=ALU.add).then_inc(s_ua.h, 16)
                        s_ua.n += 16
                    g.wait_ge(s_o1.h, 8 * bm + 2 * tau + 2)
                    s_o2.inc(g.tensor_tensor(out=fbuf[tg % 4][:], in0=osb[tg % 2][:],
                                             in1=g1b[:], op=ALU.add))
                    g.dma_start(out=y_o[128 * tg:128 * (tg + 1), :],
                                in_=fbuf[tg % 4][:]).then_inc(s_stx.h, 16)
                    s_stx.n += 16

        pending_prescale = []
        for cb in range(NCB):
            own = (cb == NCB - 1)
            for b_l in range(NB):
                gb = NB * cb + b_l
                bb = gb % 2
                tiles_l = [4 * b_l + i for i in range(4)]

                @blk.gpsimd
                def _(g, cb=cb, gb=gb, tiles_l=tiles_l):
                        if gb >= 1:
                            g.wait_ge(s_stats.h, gb)
                        for tau, tl in enumerate(tiles_l):
                            ti = 16 * cb + tl
                            g.wait_ge(s_u.h, 16 * (ti + 1))
                            ub = u_roll[ti % 5]
                            s_gsum.inc(g.tensor_tensor(out=sq_dump[tau][:], in0=ub[:],
                                                       in1=ub[:], op=ALU.mult))

                @blk.vector
                def _(v, cb=cb, b_l=b_l, gb=gb, own=own, tiles_l=tiles_l):
                    w = _vw(v, s_sv)
                    if gb >= 4:
                        v.wait_ge(s_xn.h, 4 * gb - 12)
                    for tau, tl in enumerate(tiles_l):
                        ti = 16 * cb + tl
                        v.wait_ge(s_u.h, 16 * (ti + 1))
                        v.wait_ge(s_gsum.h, 4 * gb + tau + 1)
                        ub = u_roll[ti % 5]
                        v.tensor_reduce(out=st_sum[:, tl:tl + 1], in_=ub[:], axis=AX.X, op=ALU.add)
                        wl = v.tensor_reduce(out=st_ssq[:, tl:tl + 1], in_=sq_dump[tau][:],
                                             axis=AX.X, op=ALU.add)
                    wl.then_inc(s_sv.h, 1); s_sv.n += 1
                    v.wait_ge(s_sv.h, s_sv.n)
                    w.sync()
                    C = lambda arr: arr[:, 4 * b_l:4 * b_l + 4]
                    w.tensor_scalar(out=C(st_mean), in0=C(st_sum), scalar1=1.0 / D,
                                    scalar2=None, op0=ALU.mult)
                    w.tensor_scalar(out=C(st_t1), in0=C(st_ssq), scalar1=1.0 / D,
                                    scalar2=1e-5, op0=ALU.mult, op1=ALU.add)
                    w.sync()
                    w.tensor_tensor(out=C(st_t2), in0=C(st_mean), in1=C(st_mean), op=ALU.mult)
                    w.sync()
                    w.scalar_tensor_tensor(out=C(st_den), in0=C(st_t2), scalar=-1.0,
                                           in1=C(st_t1), op0=ALU.mult, op1=ALU.add)
                    w.sync()
                    w.tensor_scalar(out=C(st_rstd), in0=C(st_den), scalar1=3.0, scalar2=-0.5,
                                    op0=ALU.subtract, op1=ALU.mult)
                    w.sync()
                    for _ in range(2):
                        w.tensor_tensor(out=C(st_t1), in0=C(st_rstd), in1=C(st_rstd), op=ALU.mult)
                        w.sync()
                        w.tensor_tensor(out=C(st_t2), in0=C(st_t1), in1=C(st_den), op=ALU.mult)
                        w.sync()
                        w.tensor_scalar(out=C(st_t2), in0=C(st_t2), scalar1=3.0, scalar2=-0.5,
                                        op0=ALU.subtract, op1=ALU.mult)
                        w.sync()
                        w.tensor_tensor(out=C(st_t1), in0=C(st_rstd), in1=C(st_t2), op=ALU.mult)
                        w.sync()
                        w.tensor_copy(out=C(st_rstd), in_=C(st_t1))
                        w.sync()
                    w.scalar_tensor_tensor(out=C(st_negmr), in0=C(st_mean), scalar=-1.0,
                                           in1=C(st_rstd), op0=ALU.mult, op1=ALU.mult)
                    w.sync()
                    v.sem_inc(s_stats.h, 1); s_stats.n += 1
                while pending_prescale and (len(pending_prescale) > 1 or gb >= 1):
                    pending_prescale.pop(0)()

                if own and b_l == 0:
                    @blk.vector
                    def _(v):
                        w = _vw(v, s_sv)
                        for j in range(NCORE):
                            sel = selm_sb[:, 32 * j:32 * (j + 1)]
                            w.tensor_tensor(out=t32a[:], in0=PrP[:], in1=sel, op=ALU.mult)
                            w.tensor_tensor(out=t32b[:], in0=PiP[:], in1=sel, op=ALU.mult)
                            w.sync()
                            va = t32a[:, :].rearrange("p (e c) -> p c e", c=4)
                            vb = t32b[:, :].rearrange("p (e c) -> p c e", c=4)
                            w.tensor_reduce(out=c4["mjr"][:], in_=va, axis=AX.X, op=ALU.add)
                            w.tensor_reduce(out=c4["mji"][:], in_=vb, axis=AX.X, op=ALU.add)
                            rr = carr_sb[:, 8 * j:8 * j + 4]
                            ri = carr_sb[:, 8 * j + 4:8 * j + 8]
                            w.sync()
                            w.tensor_tensor(out=sc["a"][:], in0=c4["mjr"][:], in1=rr, op=ALU.mult)
                            w.tensor_tensor(out=sc["b"][:], in0=c4["mji"][:], in1=ri, op=ALU.mult)
                            w.tensor_tensor(out=sc["c"][:], in0=c4["mjr"][:], in1=ri, op=ALU.mult)
                            w.tensor_tensor(out=sc["f"][:], in0=c4["mji"][:], in1=rr, op=ALU.mult)
                            w.sync()
                            w.tensor_tensor(out=c4["ta"][:], in0=sc["a"][:], in1=sc["b"][:], op=ALU.subtract)
                            w.tensor_tensor(out=c4["tb"][:], in0=sc["c"][:], in1=sc["f"][:], op=ALU.add)
                            w.sync()
                            src_r = c4["ar0"] if j % 2 == 1 else c4["ar1"]
                            dst_r = c4["ar1"] if j % 2 == 1 else c4["ar0"]
                            src_i = c4["ai0"] if j % 2 == 1 else c4["ai1"]
                            dst_i = c4["ai1"] if j % 2 == 1 else c4["ai0"]
                            if j == 0:
                                w.tensor_copy(out=dst_r[:], in_=c4["ta"][:])
                                w.tensor_copy(out=dst_i[:], in_=c4["tb"][:])
                            else:
                                w.tensor_tensor(out=dst_r[:], in0=src_r[:], in1=c4["ta"][:], op=ALU.add)
                                w.tensor_tensor(out=dst_i[:], in0=src_i[:], in1=c4["tb"][:], op=ALU.add)
                            w.sync()
                        fr = c4["ar1"] if (NCORE - 1) % 2 == 1 else c4["ar0"]
                        fi = c4["ai1"] if (NCORE - 1) % 2 == 1 else c4["ai0"]
                        w.tensor_copy(out=init_col[:, 0:4], in_=fr[:])
                        w.tensor_copy(out=init_col[:, 4:8], in_=fi[:])
                        v.drain()
                        v.sem_inc(s_set.h, 1); s_set.n += 1

                @blk.scalar
                def _(s, gb=gb, bb=bb, tiles_l=tiles_l):
                    s.wait_ge(s_stats.h, gb + 1)
                    for tau, tl in enumerate(tiles_l):
                        ti = 16 * (gb // 4) + tl
                        ub = u_roll[ti % 5]
                        s_xn.inc(s.activation(out=xn_blk[bb][tau][:], in_=ub[:], func=AF.Identity,
                                              scale=st_rstd[:, tl:tl + 1], bias=st_negmr[:, tl:tl + 1]))

                if own:
                    @blk.vector
                    def _(v, b_l=b_l, bb=bb, tiles_l=tiles_l):
                        v.wait_ge(s_xn.h, 112 + 4 * b_l + 4)
                        w = _vw(v, s_sv)
                        for tau, tl in enumerate(tiles_l):
                            w.tensor_reduce(out=st_amax[:, tl:tl + 1], in_=xn_blk[bb][tau][:],
                                            axis=AX.X, op=ALU.max, apply_absolute_value=True)
                        w.sync()
                        C = lambda arr: arr[:, 4 * b_l:4 * b_l + 4]
                        w.tensor_scalar(out=C(st_amax), in0=C(st_amax),
                                        scalar1=1e-12, scalar2=None, op0=ALU.max)
                        w.sync()
                        w.tensor_scalar(out=C(st_sff), in0=C(st_amax), scalar1=ws_col[:, 0:1],
                                        scalar2=1.0 / 112.0, op0=ALU.mult, op1=ALU.mult)
                        w.reciprocal(out=C(st_t1), in_=C(st_amax))
                        w.sync()
                        w.tensor_scalar(out=C(st_qsc), in0=C(st_t1), scalar1=224.0,
                                        scalar2=None, op0=ALU.mult)
                        w.sync()
                        v.sem_inc(s_qs.h, 1); s_qs.n += 1

                for tau, tl in enumerate(tiles_l):
                    ti = 16 * cb + tl

                    @blk.tensor
                    def _(te, ti=ti, tl=tl, bb=bb, tau=tau, own=own, b_l=b_l):
                        for grp in range(2):
                            gi = 2 * ti + grp
                            bank = ps_tx if (own or gi % 2 == 0) else ps_tq
                            te.wait_ge(s_xn.h, ti + 1)
                            if own and gi >= 1:
                                te.wait_ge(s_trE.h, gi)
                            elif gi >= 2:
                                te.wait_ge(s_trE.h, gi - 1)
                            for k in range(4):
                                dd = 4 * grp + k
                                mi = te.matmul(bank[:, 128 * k:128 * (k + 1)],
                                               xn_blk[bb][tau][:, 128 * dd:128 * (dd + 1)],
                                               ident[:], start=True, stop=True)
                            s_trT.inc(mi)
                        if own:
                            for grp in range(2):
                                gl = 2 * tl + grp
                                te.wait_ge(s_qx.h, tl + 1)
                                if gl == 0 and tl == 0:
                                    te.wait_ge(s_trE.h, 224)
                                if gl >= 1:
                                    te.wait_ge(s_tqE.h, gl)
                                if tl >= 4:
                                    te.wait_ge(s_ffn.h, 8 * (tl // 4))
                                    te.wait_ge(s_fev.h, 8 * (tl // 4))
                                for k in range(4):
                                    dd = 4 * grp + k
                                    mi = te.matmul(ps_tq[:, 128 * k:128 * (k + 1)],
                                                   qx_roll[tl % 2][:, 128 * dd:128 * (dd + 1)],
                                                   ident[:], start=True, stop=True)
                                s_tqT.inc(mi)

                    @blk.scalar
                    def _(s, ti=ti, tl=tl, bb=bb, tau=tau, own=own, b_l=b_l):
                        if own:
                            s.wait_ge(s_qs.h, b_l + 1)
                            if tl >= 2:
                                s.wait_ge(s_tqT.h, 2 * (tl - 1))
                            s_qx.inc(s.activation(out=qx_roll[tl % 2][:], in_=xn_blk[bb][tau][:],
                                                  func=AF.Identity, scale=st_qsc[:, tl:tl + 1], bias=ws_col[:, 2:3]))
                        for grp in range(2):
                            gi = 2 * ti + grp
                            bank_tr = ps_tx if (own or gi % 2 == 0) else ps_tq
                            s.wait_ge(s_trT.h, gi + 1)
                            if ti % 4 == 0 and ti >= 8 and grp == 0:
                                s.wait_ge(s_vmm.h, 4 * (ti // 4 - 1))
                            dst = xnT[(ti // 4) % 2][:, :].rearrange("p (d c) -> p d c", d=8)
                            dst = dst[:, 4 * grp:4 * grp + 4, 128 * tau:128 * (tau + 1)]
                            s_trE.inc(s.activation(out=dst, in_=bank_tr[:, :].rearrange("p (d c) -> p d c", d=4),
                                                   func=AF.Identity, scale=ws_col[:, 1:2], bias=ws_col[:, 2:3]))
                        if own:
                            for grp in range(2):
                                gl = 2 * tl + grp
                                s.wait_ge(s_tqT.h, gl + 1)
                                dst = qxT[:, :].rearrange("p (d c) -> p d c", d=8)
                                dst = dst[:, 4 * grp:4 * grp + 4, 128 * tau:128 * (tau + 1)]
                                s_tqE.inc(s.activation(out=dst, in_=ps_tq[:, :].rearrange("p (d c) -> p d c", d=4),
                                                       func=AF.Identity, scale=ws_col[:, 1:2], bias=ws_col[:, 2:3]))

                if own and b_l >= 1:
                    emit_mix(b_l - 1)

                @blk.tensor
                def _(te, gb=gb, own=own, b_l=b_l):
                    te.wait_ge(s_trE.h, 8 * (gb + 1))
                    for p in range(4):
                        gpair = 4 * gb + p
                        if own and p == 1 and b_l >= 1:
                            te.wait_ge(s_o1.h, 8 * b_l)
                        if gpair >= 2:
                            te.wait_ge(s_vev.h, 2 * (gpair - 1))
                        for half in range(2):
                            g_ = p + 4 * half
                            bank = ps_v[2 * (p % 2) + half]
                            for d_ in range(8):
                                mi = te.matmul(bank[:], bht_sb[d_][:, 128 * g_:128 * (g_ + 1)],
                                               xnT[gb % 2][:, 512 * d_:512 * (d_ + 1)],
                                               start=(d_ == 0), stop=(d_ == 7))
                        s_vmm.inc(mi)

                @blk.scalar
                def _(s, gb=gb):
                    for p in range(4):
                        gpair = 4 * gb + p
                        s.wait_ge(s_vmm.h, gpair + 1)
                        if gb >= 1:
                            s.wait_ge(s_pre.h, 4 * (gb - 1) + p + 1)
                        for half in range(2):
                            g_ = p + 4 * half
                            bank = ps_v[2 * (p % 2) + half]
                            ci = s.activation(out=vbuf[g_][:], in_=bank[:],
                                              func=AF.Identity, scale=ws_col[:, 1:2], bias=ws_col[:, 2:3])
                        s_vev.inc(ci, 2)

                def emit_prescale(cb=cb, b_l=b_l, gb=gb, own=own):
                    @blk.vector
                    def _(v, cb=cb, b_l=b_l, gb=gb, own=own):
                      for pp in range(2):
                          e_, o_ = 2 * pp, 2 * pp + 1
                          v.wait_ge(s_vev.h, 2 * (4 * gb + o_ + 1))
                          csE = ct[e_][:, BT * b_l:BT * (b_l + 1)]
                          snE = stb[e_][:, BT * b_l:BT * (b_l + 1)]
                          csO = ct[o_][:, BT * b_l:BT * (b_l + 1)]
                          snO = stb[o_][:, BT * b_l:BT * (b_l + 1)]
                          vrE, viE = vbuf[e_][:], vbuf[e_ + 4][:]
                          vrO, viO = vbuf[o_][:], vbuf[o_ + 4][:]
                          v.tensor_tensor(out=tb1[:], in0=csE, in1=vrE, op=ALU.mult)
                          v.tensor_tensor(out=tb2[:], in0=csO, in1=vrO, op=ALU.mult)
                          v.tensor_tensor(out=vt[e_][:], in0=snE, in1=viE, op=ALU.mult)
                          v.tensor_tensor(out=vt[o_][:], in0=snO, in1=viO, op=ALU.mult)
                          v.tensor_tensor(out=vt[e_][:], in0=tb1[:], in1=vt[e_][:], op=ALU.add)
                          v.tensor_tensor(out=vt[o_][:], in0=tb2[:], in1=vt[o_][:], op=ALU.add)
                          v.tensor_tensor(out=vt[e_ + 4][:], in0=snE, in1=vrE, op=ALU.mult)
                          v.tensor_tensor(out=vt[o_ + 4][:], in0=snO, in1=vrO, op=ALU.mult)
                          v.tensor_tensor(out=tb1[:], in0=csE, in1=viE, op=ALU.mult)
                          v.tensor_tensor(out=tb2[:], in0=csO, in1=viO, op=ALU.mult)
                          v.tensor_tensor(out=vt[e_ + 4][:], in0=tb1[:], in1=vt[e_ + 4][:], op=ALU.subtract)
                          s_pre.inc(v.tensor_tensor(out=vt[o_ + 4][:], in0=tb2[:], in1=vt[o_ + 4][:],
                                                    op=ALU.subtract), 2)
                      if not own:
                          for g_ in range(8):
                              ri = v.tensor_reduce(out=rtmp[:, g_:g_ + 1], in_=vt[g_][:],
                                                   axis=AX.X, op=ALU.add)
                          ri.then_inc(s_sv.h, 1); s_sv.n += 1
                          v.wait_ge(s_sv.h, s_sv.n)
                          if b_l == 0:
                              v.tensor_copy(out=racc[0][:], in_=rtmp[:])
                          elif b_l < NB - 1:
                              v.tensor_tensor(out=racc[b_l % 2][:], in0=racc[(b_l + 1) % 2][:],
                                              in1=rtmp[:], op=ALU.add)
                          else:
                              v.tensor_tensor(out=carr_sb[:, 8 * cb:8 * cb + 8],
                                              in0=racc[(b_l + 1) % 2][:], in1=rtmp[:], op=ALU.add)
                      else:
                          for g_ in range(8):
                              init = init_col[:, g_:g_ + 1] if b_l == 0 else s0[g_][:, BT - 1:BT]
                              v.tensor_tensor_scan(out=s0[g_][:], data0=vt[g_][:], data1=vt[g_][:],
                                                   initial=init, op0=ALU.add, op1=ALU.bypass)
                      if own:
                          if b_l >= 1:
                              v.wait_ge(s_mix.h, 8 * b_l)
                          for pp in range(2):
                              e_, o_ = 2 * pp, 2 * pp + 1
                              csE = ct[e_][:, BT * b_l:BT * (b_l + 1)]
                              snE = stb[e_][:, BT * b_l:BT * (b_l + 1)]
                              csO = ct[o_][:, BT * b_l:BT * (b_l + 1)]
                              snO = stb[o_][:, BT * b_l:BT * (b_l + 1)]
                              srE, siE = s0[e_][:], s0[e_ + 4][:]
                              srO, siO = s0[o_][:], s0[o_ + 4][:]
                              hrE, hiE = HT[e_][:], HT[e_ + 4][:]
                              hrO, hiO = HT[o_][:], HT[o_ + 4][:]
                              v.tensor_tensor(out=hrE, in0=snE, in1=siE, op=ALU.mult)
                              v.tensor_tensor(out=hrO, in0=snO, in1=siO, op=ALU.mult)
                              v.tensor_tensor(out=tb1[:], in0=csE, in1=srE, op=ALU.mult)
                              v.tensor_tensor(out=tb2[:], in0=csO, in1=srO, op=ALU.mult)
                              v.tensor_tensor(out=hrE, in0=tb1[:], in1=hrE, op=ALU.subtract)
                              v.tensor_tensor(out=hrO, in0=tb2[:], in1=hrO, op=ALU.subtract)
                              v.tensor_tensor(out=hiE, in0=snE, in1=srE, op=ALU.mult)
                              v.tensor_tensor(out=hiO, in0=snO, in1=srO, op=ALU.mult)
                              v.tensor_tensor(out=tb1[:], in0=csE, in1=siE, op=ALU.mult)
                              v.tensor_tensor(out=tb2[:], in0=csO, in1=siO, op=ALU.mult)
                              v.tensor_tensor(out=hiE, in0=tb1[:], in1=hiE, op=ALU.add)
                              hlast = v.tensor_tensor(out=hiO, in0=tb2[:], in1=hiO, op=ALU.add)
                          s_H.inc(hlast)

                pending_prescale.append(emit_prescale)


                if own:
                    @blk.tensor
                    def _(te, b_l=b_l):
                        for tau in range(4):
                            tl = 4 * b_l + tau
                            for h in range(2):
                                idx = 8 * b_l + 2 * tau + h
                                te.wait_ge(s_tqE.h, 2 * tl + 2)
                                if idx >= 2:
                                    te.wait_ge(s_fev.h, idx - 1)
                                fbank = ps_f if idx % 2 == 0 else ps_tq
                                for d_ in range(8):
                                    mi = te.matmul(fbank[:], qxT[:, 512 * d_ + 128 * tau:512 * d_ + 128 * (tau + 1)],
                                                   wt8_sb[d_][:, BT * h:BT * (h + 1)],
                                                   start=(d_ == 0), stop=(d_ == 7))
                                s_ffn.inc(mi)

                    @blk.scalar
                    def _(s, b_l=b_l):
                        for tau in range(4):
                            tl = 4 * b_l + tau
                            for h in range(2):
                                idx = 8 * b_l + 2 * tau + h
                                s.wait_ge(s_ffn.h, idx + 1)
                                if tl >= 4:
                                    s.wait_ge(s_o2.h, tl - 3)
                                    s.wait_ge(s_stx.h, 16 * (tl - 3))
                                s_fev.inc(s.activation(out=fbuf[tl % 4][:, BT * h:BT * (h + 1)],
                                                       in_=(ps_f if idx % 2 == 0 else ps_tq)[:],
                                                       func=AF.Identity,
                                                       scale=st_sff[:, tl:tl + 1], bias=ws_col[:, 2:3]))

                    if b_l == NB - 1:
                        while pending_prescale:
                            pending_prescale.pop(0)()
                        emit_mix(b_l)

    return nc
